# version 13
# baseline (speedup 1.0000x reference)
import contextlib

import numpy as np
import concourse.bass as bass
import concourse.mybir as mybir
from concourse.bass_utils import run_bass_kernel_spmd
from concourse.bass import broadcast_tensor_aps

F32 = mybir.dt.float32
BF16 = mybir.dt.bfloat16
I32 = mybir.dt.int32
AX = mybir.AxisListType.X
OP = mybir.AluOpType
ACT = mybir.ActivationFunctionType

B_FULL, S, D = 16, 2048, 1024
NCORES = 8
B_CORE = 2
TOK = B_CORE * S
Nd = D // 128
N_STATE = 16
K_CONV = 4
I0, I1 = 4096, 3072
H, DC, DR, HD = 16, 256, 128, 64
EPS_Q = 1e-05
LN_EPS = 1e-05
CH = 1024
FCH = 512
NBIG = 45312
NFP = 5600


class KB:
    ENGS = ("pe", "dve", "act", "pool", "sp")

    def __init__(self, nc):
        self.nc = nc
        self.q = {e: [] for e in self.ENGS}
        self.semnames = ["pe", "dve", "act", "pool", "sp", "dfw", "d0", "d1", "d2",
                         "d3", "dk0", "dk1", "dv0", "dv1", "dm", "dw", "dst"]
        self.sem = {}
        self.cnt = {n: 0 for n in self.semnames}
        self.waited = {e: {} for e in self.ENGS}

    def op(self, eng, fn, deps=(), inc=None, dinc=None):
        flat = []
        stack = list(deps)
        while stack:
            d = stack.pop()
            if d is None:
                continue
            if isinstance(d, list):
                stack.extend(d)
            else:
                flat.append(d)
        deps = flat
        if eng == "sp":
            dd = {}
            for s, v in deps:
                dd[s] = max(dd.get(s, 0), v)
            deps2 = tuple(dd.items())
        else:
            deps2 = tuple((s, v) for (s, v) in deps
                          if v > 0 and self.waited[eng].get(s, 0) < v)
        for s, v in deps2:
            self.waited[eng][s] = v
        tok, sname, amt = None, None, 0
        if inc is not None:
            self.cnt[inc] += 1
            tok = (inc, self.cnt[inc])
            sname, amt = inc, 1
        elif dinc is not None:
            if self.cnt[dinc] > 0:
                deps2 = tuple(dict(list(deps2) + [(dinc, self.cnt[dinc])]).items())
            self.cnt[dinc] += 16
            tok = (dinc, self.cnt[dinc])
            sname, amt = dinc, 16
        sem = self.sem

        def cl(e, fn=fn, deps2=deps2, sname=sname, amt=amt):
            for s, v in deps2:
                e.wait_ge(sem[s], v)
            r = fn(e)
            if sname is not None:
                assert r is not None
                r.then_inc(sem[sname], amt)
        self.q[eng].append(cl)
        return tok

    def barrier_tokens(self):
        return [(e, self.cnt[e]) for e in ("pe", "dve", "act", "pool")] + \
               [("dfw", self.cnt["dfw"]), ("dw", self.cnt["dw"])]


_CS_CACHE = {}


def _rope_tables():
    if "cs" not in _CS_CACHE:
        import jax
        import jax.numpy as jnp
        with jax.default_device(jax.devices("cpu")[0]):
            inv = 1.0 / 10000.0 ** (jnp.arange(0, DR, 2, dtype=jnp.float32) / DR)
            t = jnp.arange(S, dtype=jnp.float32)
            fr = t[:, None] * inv[None, :]
            emb = jnp.concatenate([fr, fr], axis=-1)
            _CS_CACHE["cs"] = (np.asarray(jnp.cos(emb)), np.asarray(jnp.sin(emb)))
    return _CS_CACHE["cs"]


def _strip_major(wT, strips_w=128):
    Kd, M = wT.shape
    nkc, nst = Kd // 128, M // strips_w
    a = wT.reshape(nkc, 128, nst, strips_w)
    return np.ascontiguousarray(a.transpose(2, 1, 0, 3).reshape(nst, 128, nkc * strips_w))


def host_prep(ins, core):
    f4 = np.float32
    import ml_dtypes
    bf = ml_dtypes.bfloat16
    A = lambda v: np.asarray(v)
    d = {}
    x = A(ins["L0_attn_x"])[2 * core:2 * core + 2]
    d["x_in"] = np.ascontiguousarray(x.transpose(0, 2, 1))
    d["w_inp"] = _strip_major(np.ascontiguousarray(A(ins["L0_attn_in_proj_w"]).T))
    xp = np.ascontiguousarray(A(ins["L0_attn_x_proj_w"]).T)
    xp2 = np.zeros((D, 1152), f4)
    xp2[:, :1024] = xp[:, 32:1056]
    xp2[:, 1024:1040] = xp[:, 0:16]
    xp2[:, 1040:1056] = xp[:, 16:32]
    d["w_xp"] = _strip_major(xp2)
    d["w_outp"] = _strip_major(np.ascontiguousarray(A(ins["L0_attn_out_proj_w"]).T))
    d["conv_w"] = np.ascontiguousarray(A(ins["L0_attn_conv_w"]).reshape(D, K_CONV)
                                       .reshape(Nd, 128, K_CONV).transpose(1, 0, 2)
                                       .reshape(128, Nd * K_CONV))
    d["conv_b"] = np.ascontiguousarray(A(ins["L0_attn_conv_b"]).reshape(Nd, 128).T)
    d["d_par"] = np.ascontiguousarray(A(ins["L0_attn_D_param"]).reshape(Nd, 128).T)
    d["w_g0"] = np.ascontiguousarray(A(ins["L0_ffn_w_gate"]).T.reshape(Nd, 128, I0))
    d["w_u0"] = np.ascontiguousarray(A(ins["L0_ffn_w_up"]).T.reshape(Nd, 128, I0))
    d["w_d0"] = np.ascontiguousarray(A(ins["L0_ffn_w_down"]).T.reshape(I0 // 128, 128, D))
    d["w_q"] = _strip_major(np.ascontiguousarray(A(ins["L1_attn_wq"])))
    d["w_qc"] = _strip_major(np.ascontiguousarray(A(ins["L1_attn_wqc"])))
    d["w_qr"] = _strip_major(np.ascontiguousarray(A(ins["L1_attn_wqr"])))
    d["w_qr_p"] = _strip_major(np.ascontiguousarray(np.roll(A(ins["L1_attn_wqr"]), -64, axis=1)))
    d["w_kv"] = _strip_major(np.ascontiguousarray(A(ins["L1_attn_wkv"])))
    d["w_kr"] = _strip_major(np.ascontiguousarray(A(ins["L1_attn_wkr"])))
    d["w_kr_p"] = _strip_major(np.ascontiguousarray(np.roll(A(ins["L1_attn_wkr"]), -64, axis=1)))
    d["w_kc"] = _strip_major(np.ascontiguousarray(A(ins["L1_attn_wkc"])))
    d["w_v"] = np.ascontiguousarray(A(ins["L1_attn_wv"]).reshape(2, 128, D))
    d["w_o"] = _strip_major(np.ascontiguousarray(A(ins["L1_attn_wo"])))
    for nm, key in (("b_q", "L1_attn_bq"), ("b_qc", "L1_attn_bqc"),
                    ("b_kv", "L1_attn_bkv"), ("b_kc", "L1_attn_bkc"),
                    ("b_v", "L1_attn_bv"), ("b_o", "L1_attn_bo")):
        d[nm] = np.ascontiguousarray(A(ins[key]).reshape(-1, 128))
    d["b_qr"] = np.ascontiguousarray(A(ins["L1_attn_bqr"]).reshape(1, 128))
    d["b_qr_p"] = np.ascontiguousarray(np.roll(A(ins["L1_attn_bqr"]), -64).reshape(1, 128))
    d["b_kr"] = np.ascontiguousarray(A(ins["L1_attn_bkr"]).reshape(1, 128))
    d["b_kr_p"] = np.ascontiguousarray(np.roll(A(ins["L1_attn_bkr"]), -64).reshape(1, 128))
    d["w_g1"] = np.ascontiguousarray(A(ins["L1_ffn_w_gate"]).T.reshape(Nd, 128, I1))
    d["w_u1"] = np.ascontiguousarray(A(ins["L1_ffn_w_up"]).T.reshape(Nd, 128, I1))
    d["w_d1"] = np.ascontiguousarray(A(ins["L1_ffn_w_down"]).T.reshape(I1 // 128, 128, D))
    cos_t, sin_t = _rope_tables()
    cosT = np.ascontiguousarray(cos_t.T)
    sinT = sin_t.T.copy()
    sinT[:64] *= -1.0
    d["cosT"] = cosT
    d["sinT"] = np.ascontiguousarray(sinT)
    d["tri"] = (np.arange(128)[:, None] <= np.arange(128)[None, :]).astype(f4).astype(bf)
    bo = np.zeros((Nd, 128, H), f4)
    for kc in range(Nd):
        bo[kc, 0:64, 2 * kc] = 1.0
        bo[kc, 64:128, 2 * kc + 1] = 1.0
    d["blockones"] = np.ascontiguousarray(bo.transpose(1, 0, 2).reshape(128, Nd * H)).astype(bf)
    e64 = np.zeros((65, 64), f4)
    e64[64, :] = 1.0
    d["e64"] = e64.astype(bf)
    d["pidx"] = np.tile(np.arange(32, dtype=f4)[:, None], (1, 128)).astype(bf)
    indbc = np.zeros((32, 4096), f4)
    for n in range(16):
        indbc[n, 128 * n:128 * (n + 1)] = 1.0
        indbc[16 + n, 2048 + 128 * n:2048 + 128 * (n + 1)] = 1.0
    d["indbc"] = indbc.astype(bf)
    e8 = np.zeros((8, 8, 128), f4)
    for j in range(8):
        e8[j, j, :] = 1.0
    d["e8"] = np.ascontiguousarray(e8.transpose(1, 0, 2).reshape(8, 8 * 128))
    d["ident"] = np.eye(128, dtype=f4).astype(bf)
    d["ident_f"] = np.eye(128, dtype=f4)
    return d


class Prog:
    def __init__(self):
        nc = bass.Bass("TRN2", target_bir_lowering=False, debug=False,
                       dynamic_dma_scratch_size=8192)
        self.nc = nc
        self.k = KB(nc)
        self.D = {}
        self._mkio()
        self._mksbuf()
        self.t_carry = {}
        self.t_tail = [None] * Nd
        self.t_stripbuf = {}
        self.t_sc_guard = {}
        self.t_out_last = None
        self.t_fw = None
        self.bar = []

    def _mkio(self):
        nc = self.nc

        def din(name, shape, dt=F32):
            self.D[name] = nc.dram_tensor("in_" + name, list(shape), dt, kind="ExternalInput").ap()
        din("x_in", (B_CORE, D, S))
        din("w_inp", (16, 128, 1024)); din("w_xp", (9, 128, 1024)); din("w_outp", (8, 128, 1024))
        din("conv_w", (128, Nd * K_CONV)); din("conv_b", (128, Nd)); din("d_par", (128, Nd))
        din("w_g0", (Nd, 128, I0)); din("w_u0", (Nd, 128, I0)); din("w_d0", (I0 // 128, 128, D))
        din("w_q", (2, 128, 1024)); din("w_qc", (7, 128, 256)); din("w_qr", (1, 128, 1024))
        din("w_qr_p", (1, 128, 1024)); din("w_kv", (2, 128, 1024)); din("w_kr", (1, 128, 1024))
        din("w_kr_p", (1, 128, 1024)); din("w_kc", (7, 128, 256)); din("w_v", (2, 128, D))
        din("w_o", (8, 128, 1024))
        din("b_q", (2, 128)); din("b_qc", (7, 128)); din("b_qr", (1, 128)); din("b_qr_p", (1, 128))
        din("b_kv", (2, 128)); din("b_kr", (1, 128)); din("b_kr_p", (1, 128)); din("b_kc", (7, 128))
        din("b_v", (8, 128)); din("b_o", (8, 128))
        din("w_g1", (Nd, 128, I1), I32); din("w_u1", (Nd, 128, I1), I32)
        din("w_d1", (I1 // 128, 128, D), I32)
        din("cosT", (128, S), F32); din("sinT", (128, S), F32); din("tri", (128, 128), BF16)
        din("blockones", (128, Nd * H), BF16); din("e64", (65, 64), BF16)
        din("e8", (8, 8 * 128), F32); din("ident", (128, 128), BF16)
        din("pidx", (32, 128), BF16); din("indbc", (32, 4096), BF16)
        din("ident_f", (128, 128), F32)
        self.out_h = nc.dram_tensor("out", [B_CORE, D, S], F32, kind="ExternalOutput").ap()
        scr = {}
        scr["g0"] = nc.dram_tensor("scr_g0", [Nd, 128, I0], BF16, kind="Internal").ap()
        scr["u0"] = nc.dram_tensor("scr_u0", [Nd, 128, I0], BF16, kind="Internal").ap()
        scr["d0"] = nc.dram_tensor("scr_d0", [I0 // 128, 128, D], BF16, kind="Internal").ap()
        scr["g1"] = nc.dram_tensor("scr_g1", [Nd, 128, I1], BF16, kind="Internal").ap()
        scr["u1"] = nc.dram_tensor("scr_u1", [Nd, 128, I1], BF16, kind="Internal").ap()
        scr["d1"] = nc.dram_tensor("scr_d1", [I1 // 128, 128, D], BF16, kind="Internal").ap()
        for nm, shp in (("inp", (16, 128, 1024)), ("xp", (9, 128, 1024)),
                        ("outp", (8, 128, 1024)), ("q", (2, 128, 1024)),
                        ("qc", (7, 128, 256)), ("qr", (1, 128, 1024)),
                        ("qr_p", (1, 128, 1024)), ("kv", (2, 128, 1024)),
                        ("kc", (7, 128, 256)), ("kr", (1, 128, 1024)),
                        ("kr_p", (1, 128, 1024)), ("v", (2, 128, 1024)),
                        ("o", (8, 128, 1024))):
            scr[nm] = nc.dram_tensor("scr_" + nm, list(shp), BF16, kind="Internal").ap()
        scr["kk"] = nc.dram_tensor("scr_kk", [8, 128, 2048], BF16, kind="Internal").ap()
        scr["vv"] = nc.dram_tensor("scr_vv", [16, 128, 1040], BF16, kind="Internal").ap()
        self.scr = scr

    def _mksbuf(self):
        nc = self.nc
        self.X = [nc.alloc_sbuf_tensor(f"X{i}", [128, TOK], BF16).ap() for i in range(Nd)]
        self.BIG = nc.alloc_sbuf_tensor("BIGP", [128, NBIG], BF16).ap()
        self.FP = nc.alloc_sbuf_tensor("FPP", [128, NFP], F32).ap()
        self.IP = nc.alloc_sbuf_tensor("IPP", [128, 1024], I32).ap()
        for nm, shape, dt in (("tri", [128, 128], BF16), ("blockones", [128, Nd * H], BF16),
                              ("e64", [65, 64], BF16), ("e8", [8, 8 * 128], F32),
                              ("ident", [128, 128], BF16), ("ident_f", [128, 128], F32),
                              ("conv_w", [128, Nd * K_CONV], F32), ("conv_b", [128, Nd], F32),
                              ("d_par", [128, Nd], F32), ("bias_s", [128, 40], F32),
                              ("ones_cb", [128, 1], BF16), ("ones_cf", [128, 1], F32),
                              ("ones_rf", [1, 128], F32),
                              ("ones_rb", [1, 128], BF16),
                              ("ind32", [32, 128], BF16), ("pidx", [32, 128], BF16),
                              ("indbc", [32, 4096], BF16),
                              ("probe", [128, 1], F32),
                              ("probe_i", [128, 1], I32), ("shift", [128, 1], F32),
                              ("hcar", [128, Nd * N_STATE], BF16), ("trcp", [65, 512], BF16),
                              ("rowt", [1, 16], F32), ("negm", [128, 16], F32),
                              ("knmax", [16, 2], F32), ("qnmax", [16, 1], F32),
                              ("mrow", [16, 1], F32), ("fconst", [128, 2], F32)):
            setattr(self, nm, nc.alloc_sbuf_tensor(nm, shape, dt).ap())
        self.psum = [nc.alloc_psum_tensor(f"ps{i}", [128, 512], F32).ap() for i in range(8)]
        o = 0

        def al(n):
            nonlocal o
            r = o
            o += n
            assert o <= NBIG, o
            return r
        self.A_XCW = [al(1027) for _ in range(Nd)]
        self.A_XC = [al(CH) for _ in range(Nd)]
        self.A_Z = [al(CH) for _ in range(Nd)]
        self.A_W = [al(CH) for _ in range(Nd)]
        self.A_SC = [al(CH) for _ in range(4)]
        self.A_BC = [al(CH) for _ in range(2)]
        self.A_WS = [al(1024) for _ in range(2)]
        self.A_WS2 = [al(1024) for _ in range(2)]
        self.A_SG = al(512)
        self.A_Q = al(512)
        self.A_MISC = al(64)
        self.A_BCS = al(1024)
        self.A_TL = al(64)
        self.big_used = o
        self.R0 = self.A_XCW[0]
        fo = 0

        def fl(n):
            nonlocal fo
            r = fo
            fo += n
            assert fo <= NFP, fo
            return r
        self.F_ROW = fl(192)
        self.F_SC = [fl(512) for _ in range(3)]
        self.F_Y = [fl(1024) for _ in range(2)]
        self.F_W = fl(1024)
        self.F_W2 = fl(512)

    def mm(self, deps, out, lhsT, rhs, start, stop, inc=None):
        return self.k.op("pe", lambda e: e.matmul(out, lhsT, rhs, start=start, stop=stop),
                         deps=deps, inc=inc)

    def dma(self, deps, out, in_, sem="dfw", eng="sp", allow_nc=False):
        if allow_nc:
            return self.k.op(eng, lambda e: e.dma_start(out=out, in_=in_,
                                                        allow_slow_non_contiguous=True),
                             deps=deps, dinc=sem)
        return self.k.op(eng, lambda e: e.dma_start(out=out, in_=in_), deps=deps, dinc=sem)

    def B(self, c0, w):
        return self.BIG[:, c0:c0 + w]

    @staticmethod
    def badd(e, out, in0, bias_ap):
        a, b = broadcast_tensor_aps(in0, bias_ap)
        return e.tensor_add(out, a, b)

    def F(self, c0, w):
        return self.FP[:, c0:c0 + w]

    def stream_strip_f32(self, deps, src_ap, width, slot):
        k = self.k
        base = (self.A_WS[slot] if slot < 2 else self.A_WS2[slot - 2])
        dst = self.B(base, width)
        prev = self.t_stripbuf.get(slot)
        toks = []
        for piece in range(0, width, 1024):
            w = min(1024, width - piece)
            tl = self.dma(list(deps) + [prev, self.t_fw] + toks[-1:], self.F(self.F_W, w),
                          src_ap[:, piece:piece + w])
            tc = k.op("act", lambda e, piece=piece, w=w, dst=dst: e.copy(
                dst[:, piece:piece + w], self.F(self.F_W, w)), deps=[tl], inc="act")
            self.t_fw = tc
            toks.append(tc)
        return dst, toks[-1]

    def stream_strip_bf16(self, deps, scr_ap, kc0, nkc, m, slot):
        base = (self.A_WS[slot] if slot < 2 else self.A_WS2[slot - 2])
        dst = self.B(base, nkc * 128)
        prev = self.t_stripbuf.get(slot)
        srcv = scr_ap[kc0:kc0 + nkc, :, 128 * m:128 * (m + 1)].rearrange("a p j -> p a j")
        dstv = dst.rearrange("p (a j) -> p a j", j=128)
        tl = self.dma(list(deps) + [prev], dstv, srcv, sem="d%d" % slot, eng="sp")
        return dst, tl

    def stream_w(self, deps, name, strip, width, slot):
        base = (self.A_WS[slot] if slot < 2 else self.A_WS2[slot - 2])
        dst = self.B(base, width)
        prev = self.t_stripbuf.get(slot)
        tl = self.dma(list(deps) + [prev, self.t_wconv[name]], dst,
                      self.scr[name][strip][:, 0:width], sem="d%d" % slot, eng="sp")
        return dst, tl

    def strip_done(self, slot, tok):
        self.t_stripbuf[slot] = tok

    def emit_init(self):
        k, D_ = self.k, self.D
        tc = []
        for nm in ("tri", "blockones", "e64", "e8", "ident", "ident_f", "pidx", "indbc",
                   "conv_w", "conv_b", "d_par"):
            tc.append(self.dma([], getattr(self, nm)[:, :], D_[nm][:, :]))
        self.bias_cols = {"b_q": 0, "b_qc": 2, "b_qr": 9, "b_qr_p": 10, "b_kv": 11, "b_kr": 13,
                          "b_kr_p": 14, "b_kc": 15, "b_v": 22, "b_o": 30}
        for nm, c0 in self.bias_cols.items():
            n = self.D[nm].shape[0]
            tc.append(self.dma([], self.bias_s[:, c0:c0 + n], D_[nm].rearrange("a b -> b a"),
                                  allow_nc=True))
        self.t_const = tc[-1]
        p1 = k.op("dve", lambda e: e.memset(self.probe[:, :], 0.7), inc="dve")
        p2 = k.op("dve", lambda e: e.tensor_copy(self.probe_i[:, :], self.probe[:, :]),
                  deps=[p1], inc="dve")
        p3 = k.op("dve", lambda e: e.tensor_copy(self.probe[:, :], self.probe_i[:, :]),
                  deps=[p2], inc="dve")
        self.t_shift = k.op("dve", lambda e: e.tensor_scalar(self.shift[:, :], self.probe[:, :],
                                                             -0.5, 0.5, OP.mult, OP.add),
                            deps=[p3], inc="dve")
        k.op("dve", lambda e: e.memset(self.ones_cb[:, :], 1.0), inc="dve")
        k.op("dve", lambda e: e.memset(self.ones_cf[:, :], 1.0), inc="dve")
        k.op("dve", lambda e: e.memset(self.ones_rb[:, :], 1.0), inc="dve")
        self.t_ones = k.op("dve", lambda e: e.memset(self.ones_rf[:, :], 1.0), inc="dve")
        self.t_trcp0 = k.op("dve", lambda e: e.memset(self.trcp[0:64, :], 0.0), inc="dve")
        k.op("dve", lambda e: e.memset(self.fconst[:, 0:1], float(LN_EPS)), inc="dve")
        self.t_fc = k.op("dve", lambda e: e.memset(self.fconst[:, 1:2], 1.0), inc="dve")
        self.t_x = [[None] * Nd for _ in range(B_CORE)]
        stg = [self.F(self.F_W, 1024), self.F(self.F_Y[0], 1024)]
        tstg = [self.t_const, self.t_const]
        idx = 0
        for b in range(B_CORE):
            for i in range(Nd):
                th = None
                for half in range(2):
                    ci = idx % 2
                    idx += 1
                    tl = self.dma([tstg[ci]], stg[ci],
                                  D_["x_in"][b, 128 * i:128 * (i + 1), 1024 * half:1024 * (half + 1)],
                                  sem="d%d" % (2 + ci), eng="act")
                    th = k.op("act", lambda e, b=b, i=i, half=half, ci=ci: e.copy(
                        self.X[i][:, b * S + 1024 * half:b * S + 1024 * (half + 1)], stg[ci]),
                        deps=[tl, th], inc="act")
                    tstg[ci] = th
                self.t_x[b][i] = th
        self.t_stg = tstg
        self.t_xload = th
        self.bar = k.barrier_tokens()

    def emit_weight_prep(self):
        k, D_ = self.k, self.D
        stg = [self.F(self.F_W, 1024), self.F(self.F_Y[0], 1024)]
        qb = [self.B(self.A_WS[0], 1024), self.B(self.A_WS[1], 1024)]
        gb = [self.B(self.A_WS2[0], 1024), self.B(self.A_WS2[1], 1024)]
        tstg = list(self.t_stg)
        tq = [None, None]
        tgb = [None, None]
        tst = [None, None]
        self.t_wconv = {}
        self.t_prep_ps = None
        idx = 0

        def pieces(n_k, n_m, pw=1024):
            out = []
            for kc in range(n_k):
                for p0 in range(0, n_m, pw):
                    out.append((kc, p0, min(pw, n_m - p0)))
            return out

        def conv_f32(nm, src):
            nonlocal idx
            dst = self.scr[nm]
            n_k, _, n_m = dst.shape
            for kc, p0, w in pieces(n_k, n_m):
                ci = idx % 2
                idx += 1
                tl = self.dma([tstg[ci]], stg[ci][:, 0:w], src[kc, :, p0:p0 + w],
                              sem="d%d" % ci, eng="act")
                tcv = k.op("act", lambda e, w=w, ci=ci: e.copy(qb[ci][:, 0:w], stg[ci][:, 0:w]),
                           deps=[tl, tst[ci], tq[ci]], inc="act")
                tstg[ci] = tcv
                tq[ci] = tcv
                tst[ci] = self.dma([tcv], dst[kc, :, p0:p0 + w], qb[ci][:, 0:w],
                                   sem=("dw" if ci == 0 else "dm"))
            self.t_wconv[nm] = tst[(idx - 1) % 2]

        def quant_l0(src, dst, n_k, n_m, col):
            nonlocal idx
            acc = self.F(self.F_ROW + 16, 1)
            nacc = [self.F(self.F_ROW + 17, 1), self.F(self.F_ROW + 12, 1)]
            t_acc = None
            tr = [None, None]
            for kc, p0, w in pieces(n_k, n_m):
                ci = idx % 2
                idx += 1
                tl = self.dma([tstg[ci]], stg[ci][:, 0:w], src[kc, :, p0:p0 + w],
                              sem="d%d" % ci, eng="act")
                tr[ci] = k.op("dve", lambda e, ci=ci, w=w: e.reduce_sum(
                    nacc[ci], stg[ci][:, 0:w], AX, apply_absolute_value=True),
                    deps=[tl, t_acc], inc="dve")
                if t_acc is None:
                    t_acc = k.op("dve", lambda e, ci=ci: e.tensor_copy(acc, nacc[ci]),
                                 deps=[tr[ci]], inc="dve")
                else:
                    t_acc = k.op("dve", lambda e, ci=ci: e.tensor_add(acc, acc, nacc[ci]),
                                 deps=[tr[ci]], inc="dve")
                tstg[ci] = tr[ci]
            tm = self.mm([t_acc, self.t_ones, self.t_prep_ps], self.psum[7][0:1, 0:1], acc,
                         self.ones_cf[:, :], True, True, inc="pe")
            tmean = k.op("dve", lambda e: e.tensor_scalar(self.F(self.F_ROW + 18, 1)[0:1, :],
                                                          self.psum[7][0:1, 0:1],
                                                          1.0 / (n_k * 128 * n_m), EPS_Q,
                                                          OP.mult, OP.max),
                         deps=[tm], inc="dve")
            tbb = self.mm([tmean], self.psum[7][:, 1:2], self.ones_rf[:, :],
                          self.F(self.F_ROW + 18, 1)[0:1, :], True, True, inc="pe")
            t_sw = k.op("dve", lambda e, col=col: e.tensor_copy(self.F(self.F_ROW + 20 + col, 1),
                                                                self.psum[7][:, 1:2]),
                        deps=[tbb], inc="dve")
            t_swr = k.op("dve", lambda e, col=col: e.reciprocal(self.F(self.F_ROW + 26 + col, 1),
                                                                self.F(self.F_ROW + 20 + col, 1)),
                         deps=[t_sw], inc="dve")
            self.t_prep_ps = t_swr
            for kc, p0, w in pieces(n_k, n_m):
                ci = idx % 2
                idx += 1
                tl = self.dma([tstg[ci]], stg[ci][:, 0:w], src[kc, :, p0:p0 + w],
                              sem="d%d" % ci, eng="act")
                tt = k.op("dve", lambda e, col=col, ci=ci, w=w: e.tensor_scalar(
                    stg[ci][:, 0:w], stg[ci][:, 0:w], self.F(self.F_ROW + 26 + col, 1),
                    None, OP.mult),
                    deps=[tl, t_swr], inc="dve")
                tg = k.op("act", lambda e, ci=ci, w=w: e.activation(gb[ci][:, 0:w],
                                                                    stg[ci][:, 0:w], ACT.Sign),
                          deps=[tt, tgb[ci]], inc="act")
                ta = k.op("dve", lambda e, ci=ci, w=w: e.tensor_mul(stg[ci][:, 0:w],
                                                                    stg[ci][:, 0:w],
                                                                    gb[ci][:, 0:w]),
                          deps=[tg], inc="dve")
                tge = k.op("dve", lambda e, ci=ci, w=w: e.tensor_scalar(stg[ci][:, 0:w],
                                                                        stg[ci][:, 0:w],
                                                                        0.5, None, OP.is_ge),
                           deps=[ta], inc="dve")
                tqm = k.op("dve", lambda e, ci=ci, w=w: e.tensor_mul(qb[ci][:, 0:w],
                                                                     stg[ci][:, 0:w],
                                                                     gb[ci][:, 0:w]),
                           deps=[tge, tst[ci], tq[ci]], inc="dve")
                tstg[ci] = tqm
                tgb[ci] = tqm
                tq[ci] = tqm
                tst[ci] = self.dma([tqm], dst[kc, :, p0:p0 + w], qb[ci][:, 0:w],
                                   sem=("dw" if ci == 0 else "dm"))
            return tst[(idx - 1) % 2]

        ib = [self.IP[:, 0:512], self.IP[:, 512:1024]]
        tib = [None, None]

        def conv_l1(src, dst, n_k, n_m):
            nonlocal idx
            for kc, p0, w in pieces(n_k, n_m, 512):
                ci = idx % 2
                idx += 1
                tl = self.dma([tib[ci]], ib[ci], src[kc, :, p0:p0 + 512],
                              sem="d%d" % (2 + ci), eng="act")
                tcv = k.op("dve", lambda e, ci=ci: e.tensor_copy(qb[ci][:, 0:512], ib[ci]),
                           deps=[tl, tst[ci], tq[ci]], inc="dve")
                tib[ci] = tcv
                tq[ci] = tcv
                tst[ci] = self.dma([tcv], dst[kc, :, p0:p0 + 512], qb[ci][:, 0:512],
                                   sem=("dw" if ci == 0 else "dm"))
            return tst[(idx - 1) % 2]

        conv_f32("inp", D_["w_inp"])
        conv_f32("xp", D_["w_xp"])
        conv_f32("outp", D_["w_outp"])
        conv_f32("q", D_["w_q"])
        conv_f32("qc", D_["w_qc"])
        conv_f32("qr", D_["w_qr"])
        conv_f32("qr_p", D_["w_qr_p"])
        conv_f32("kv", D_["w_kv"])
        conv_f32("kc", D_["w_kc"])
        conv_f32("kr", D_["w_kr"])
        conv_f32("kr_p", D_["w_kr_p"])
        conv_f32("v", D_["w_v"])
        conv_f32("o", D_["w_o"])
        t1 = quant_l0(D_["w_g0"], self.scr["g0"], Nd, I0, 0)
        t2 = quant_l0(D_["w_u0"], self.scr["u0"], Nd, I0, 1)
        t3 = quant_l0(D_["w_d0"], self.scr["d0"], I0 // 128, D, 2)
        t4 = conv_l1(D_["w_g1"], self.scr["g1"], Nd, I1)
        t5 = conv_l1(D_["w_u1"], self.scr["u1"], Nd, I1)
        t6 = conv_l1(D_["w_d1"], self.scr["d1"], I1 // 128, D)
        self.t_scr = {"g0": t1, "u0": t2, "d0": t3, "g1": t4, "u1": t5, "d1": t6}
        self.t_wconv.update(self.t_scr)
        self.t_stripbuf[0] = tst[0]
        self.t_stripbuf[1] = tst[1]
        self.t_stripbuf[2] = tgb[0]
        self.t_stripbuf[3] = tgb[1]
        self.t_fw = k.op("dve", lambda e: e.memset(self.F(self.F_ROW + 13, 1), 0.0),
                         deps=[tstg[0], tstg[1], tq[0], tq[1], tgb[0], tgb[1],
                               tst[0], tst[1], self.t_prep_ps], inc="dve")

    def emit_layernorm(self, deps, xs_tiles, x_out_tiles, f32_dma=None):
        k = self.k
        r0 = self.F(self.F_SC[0], 512)[0:1, :]
        r1 = self.F(self.F_SC[1], 512)[0:1, :]
        r2 = self.F(self.F_SC[2], 512)[0:1, :]
        tsum = None
        for i in range(Nd):
            dd = (list(deps) + list(self.t_ln_ps) + [self.t_fw]) if i == 0 else [tsum]
            tsum = self.mm(dd, self.psum[6][0:1, :], self.ones_cb[:, :], xs_tiles[i],
                           start=(i == 0), stop=(i == Nd - 1), inc="pe")
        tss = None
        tg = None
        for i in range(Nd):
            tg = k.op("act", lambda e, i=i: e.activation(self.B(self.A_SG, 512), xs_tiles[i],
                                                         ACT.Square),
                      deps=list(deps) + [tg, tss, self.t_fw if i == 0 else None], inc="act")
            tss = self.mm([tg], self.psum[7][0:1, :], self.ones_cb[:, :], self.B(self.A_SG, 512),
                          start=(i == 0), stop=(i == Nd - 1), inc="pe")
        t_m = k.op("dve", lambda e: e.tensor_scalar(r0, self.psum[6][0:1, :], 1.0 / D, None, OP.mult),
                   deps=[tsum, self.t_ln_app], inc="dve")
        t_q = k.op("dve", lambda e: e.tensor_scalar(r1, self.psum[7][0:1, :], 1.0 / D, None, OP.mult),
                   deps=[tss, self.t_ln_app], inc="dve")
        t_m2 = k.op("dve", lambda e: e.tensor_mul(r2, r0, r0), deps=[t_m], inc="dve")
        t_v = k.op("dve", lambda e: e.tensor_sub(r1, r1, r2), deps=[t_q, t_m2], inc="dve")
        t_sd = k.op("act", lambda e: e.activation(r1, r1, ACT.Sqrt, bias=self.fconst[0:1, 0:1]),
                    deps=[t_v, self.t_fc], inc="act")
        t_a = k.op("dve", lambda e: e.reciprocal(r1, r1), deps=[t_sd], inc="dve")
        t_b = k.op("dve", lambda e: e.scalar_tensor_tensor(r2, r0, -1.0, r1, OP.mult, OP.mult),
                   deps=[t_m, t_a], inc="dve")
        tA = self.mm([t_a], self.psum[6][:, :], self.ones_rf[:, :], r1, True, True, inc="pe")
        tB = self.mm([t_b, tA], self.psum[7][:, :], self.ones_rf[:, :], r2, True, True, inc="pe")
        abc = self.F(self.F_SC[0], 512)
        bbc = self.F(self.F_SC[1], 512)
        te1 = k.op("dve", lambda e: e.tensor_copy(abc, self.psum[6][:, :]),
                   deps=[tA, t_b], inc="dve")
        te2 = k.op("dve", lambda e: e.tensor_copy(bbc, self.psum[7][:, :]), deps=[tB], inc="dve")
        self.t_ln_ps = (te1, te2)
        touts = []
        tp = None
        for i in range(Nd):
            tt = k.op("dve", lambda e, i=i: e.tensor_mul(self.F(self.F_Y[0], 512), xs_tiles[i], abc),
                      deps=[te1, tp], inc="dve")
            if f32_dma is None:
                t2 = k.op("dve", lambda e, i=i: e.tensor_add(x_out_tiles[i], self.F(self.F_Y[0], 512),
                                                             bbc),
                          deps=[tt, te2], inc="dve")
                touts.append(t2)
                tp = t2
            else:
                b, t0 = f32_dma
                fo = self.F(self.F_Y[1], 512)
                t2 = k.op("dve", lambda e: e.tensor_add(fo, self.F(self.F_Y[0], 512), bbc),
                          deps=[tt, te2, self.t_out_last], inc="dve")
                t3 = self.dma([t2], self.out_h[b, 128 * i:128 * (i + 1), t0:t0 + 512], fo, sem="dst")
                self.t_out_last = t3
                touts.append(t3)
                tp = t2
        self.t_ln_app = tp
        return touts

    def emit_col_scales(self, deps, tiles, n_tiles, save_c2=False):
        k = self.k
        mx = self.B(self.A_SC[0], 512)
        mn = self.B(self.A_SC[1], 512)
        t1 = k.op("dve", lambda e: e.tensor_max(mx, tiles[0], tiles[1]), deps=list(deps), inc="dve")
        t2 = k.op("dve", lambda e: e.tensor_tensor(mn, tiles[0], tiles[1], OP.min),
                  deps=list(deps) + [t1], inc="dve")
        for i in range(2, n_tiles):
            t1 = k.op("dve", lambda e, i=i: e.tensor_max(mx, mx, tiles[i]), deps=[t1], inc="dve")
            t2 = k.op("dve", lambda e, i=i: e.tensor_tensor(mn, mn, tiles[i], OP.min),
                      deps=[t2], inc="dve")
        tam = k.op("dve", lambda e: e.scalar_tensor_tensor(mx, mn, -1.0, mx, OP.mult, OP.max),
                   deps=[t1, t2], inc="dve")
        amax_c = self.F(self.F_ROW, 4)
        tcols = []
        for j in range(4):
            tcv = k.op("dve", lambda e, j=j: e.tensor_copy(self.F(self.F_W, 128),
                                                           mx[:, 128 * j:128 * (j + 1)]),
                       deps=[tam, self.t_fw] + tcols[-1:], inc="dve")
            self.t_fw = tcv
            ttr = self.k.op("pe", lambda e: e.transpose(
                self.psum[4][:, 0:128], self.F(self.F_W, 128), self.ident_f[:, :]),
                deps=[tcv], inc="pe")
            tr = k.op("dve", lambda e, j=j: e.reduce_max(amax_c[:, j:j + 1],
                                                         self.psum[4][:, 0:128], AX),
                      deps=[ttr], inc="dve")
            self.t_fw = tr
            tcols.append(tr)
        tclip = k.op("dve", lambda e: e.tensor_scalar(amax_c, amax_c, EPS_Q, None, OP.max),
                     deps=tcols, inc="dve")
        s_col = self.F(self.F_ROW + 4, 4)
        c_col = self.F(self.F_ROW + 8, 4)
        ts1 = k.op("dve", lambda e: e.reciprocal(s_col, amax_c), deps=[tclip], inc="dve")
        ts2 = k.op("dve", lambda e: e.tensor_scalar(s_col, s_col, 127.0, None, OP.mult),
                   deps=[ts1], inc="dve")
        tc1 = k.op("dve", lambda e: e.tensor_scalar(c_col, amax_c, 1.0 / 127.0, None, OP.mult),
                   deps=[tclip], inc="dve")
        pk = self.F(self.F_ROW + 32, 8)
        tp1 = k.op("dve", lambda e: e.tensor_copy(pk[:, 0:4], s_col), deps=[ts2], inc="dve")
        tp2 = k.op("dve", lambda e: e.tensor_copy(pk[:, 4:8], c_col), deps=[tc1, tp1], inc="dve")
        ttr2 = self.k.op("pe", lambda e: e.transpose(self.psum[4][0:8, 0:128], pk,
                                                     self.ident_f[:, :]),
                         deps=[tp2], inc="pe")
        rows = self.F(self.F_ROW + 40, 128)[0:8, :]
        tev = k.op("dve", lambda e: e.tensor_copy(rows, self.psum[4][0:8, 0:128]),
                   deps=[ttr2], inc="dve")
        c_dst = self.F_SC[2] if save_c2 else self.F_SC[1]
        t_s = t_c = None
        for j in range(4):
            tmS = self.mm([tev, t_s], self.psum[5][:, 128 * j:128 * (j + 1)],
                          self.e8[:, 128 * j:128 * (j + 1)], rows, True, True, inc="pe")
            t_s = k.op("dve", lambda e, j=j: e.tensor_copy(self.F(self.F_SC[0] + 128 * j, 128),
                                                           self.psum[5][:, 128 * j:128 * (j + 1)]),
                       deps=[tmS], inc="dve")
        for j in range(4):
            tmC = self.mm([tev, t_s, t_c], self.psum[5][:, 128 * j:128 * (j + 1)],
                          self.e8[:, 128 * (4 + j):128 * (4 + j + 1)], rows, True, True, inc="pe")
            t_c = k.op("dve", lambda e, j=j, c_dst=c_dst: e.tensor_copy(
                self.F(c_dst + 128 * j, 128), self.psum[5][:, 128 * j:128 * (j + 1)]),
                deps=[tmC], inc="dve")
        return t_s, t_c

    def emit_quant(self, deps, in_tiles, out_tiles, n_tiles, t_s):
        k = self.k
        toks = []
        tprev = None
        for i in range(n_tiles):
            tt = k.op("dve", lambda e, i=i: e.tensor_mul(self.F(self.F_W, 512), in_tiles[i],
                                                         self.F(self.F_SC[0], 512)),
                      deps=list(deps) + [t_s, tprev, self.t_fw], inc="dve")
            self.t_fw = tt
            tg = k.op("act", lambda e: e.activation(self.B(self.A_SG, 512), self.F(self.F_W, 512),
                                                    ACT.Sign),
                      deps=[tt], inc="act")
            ti = k.op("dve", lambda e: e.scalar_tensor_tensor(self.IP[:, 0:512],
                                                              self.B(self.A_SG, 512),
                                                              self.shift[:, :],
                                                              self.F(self.F_W, 512),
                                                              OP.mult, OP.add),
                      deps=[tg, self.t_shift], inc="dve")
            tq = k.op("dve", lambda e, i=i: e.tensor_scalar(out_tiles[i], self.IP[:, 0:512],
                                                            -128.0, 127.0, OP.max, OP.min),
                      deps=[ti], inc="dve")
            toks.append(tq)
            tprev = tq
            self.t_fw = tq
        return toks

    def emit_l0_chunk(self, b, c):
        k, D_ = self.k, self.D
        win = b * S + c * CH
        bar = self.bar
        evac = {}
        for m in range(16):
            ws, tw = self.stream_w(bar + [self.t_const], "inp", m, 1024, m % 2)
            for n2 in range(2):
                pt = self.psum[(m * 2 + n2) % 4]
                lastmm = None
                for kc in range(Nd):
                    lastmm = self.mm([tw, evac.get((m - 2, n2))] + bar, pt[:, :],
                                     ws[:, kc * 128:(kc + 1) * 128],
                                     self.X[kc][:, win + 512 * n2:win + 512 * (n2 + 1)],
                                     start=(kc == 0), stop=(kc == Nd - 1),
                                     inc="pe" if kc == Nd - 1 else None)
                if m < 8:
                    te = k.op("dve", lambda e, m=m, n2=n2, pt=pt: e.tensor_copy(
                        self.B(self.A_XCW[m] + 3 + 512 * n2, 512), pt[:, :]),
                        deps=[lastmm] + bar, inc="dve")
                else:
                    te = k.op("act", lambda e, m=m, n2=n2, pt=pt: e.activation(
                        self.B(self.A_Z[m - 8] + 512 * n2, 512), pt[:, :], ACT.Silu),
                        deps=[lastmm] + bar, inc="act")
                evac[(m, n2)] = te
            self.strip_done(m % 2, lastmm)
        conv_toks = []
        t_acc_last = None
        for i in range(Nd):
            base = self.A_XCW[i]
            if c == 0:
                tp = k.op("dve", lambda e, base=base: e.memset(self.B(base, 3), 0.0),
                          deps=[evac[(i, 0)]] + bar, inc="dve")
            else:
                tp = k.op("dve", lambda e, base=base, i=i: e.tensor_copy(
                    self.B(base, 3), self.B(self.A_TL + 8 * i, 3)),
                    deps=[self.t_tail[i], evac[(i, 0)]], inc="dve")
            acc = self.F(self.F_Y[1], CH)
            t0 = k.op("dve", lambda e, base=base, i=i: e.tensor_scalar(
                acc, self.B(base, CH), self.conv_w[:, i * 4:i * 4 + 1], None, OP.mult),
                deps=[tp, evac[(i, 0)], evac[(i, 1)], self.t_const, t_acc_last], inc="dve")
            for j in range(1, K_CONV):
                t0 = k.op("dve", lambda e, base=base, i=i, j=j: e.scalar_tensor_tensor(
                    acc, self.B(base + j, CH), self.conv_w[:, i * 4 + j:i * 4 + j + 1], acc,
                    OP.mult, OP.add), deps=[t0], inc="dve")
            tt = k.op("dve", lambda e, base=base, i=i: e.tensor_copy(
                self.B(self.A_TL + 8 * i, 3), self.B(base + CH, 3)), deps=[t0], inc="dve")
            self.t_tail[i] = tt
            ts = k.op("act", lambda e, i=i: e.activation(self.B(self.A_XC[i], CH), acc,
                                                         ACT.Silu, bias=self.conv_b[:, i:i + 1]),
                      deps=[t0] + bar, inc="act")
            t_acc_last = ts
            conv_toks.append(ts)
        dt_toks = [None] * Nd
        t_bc = None
        for m in range(9):
            ws, tw = self.stream_w([self.t_const], "xp", m, 1024, m % 2)
            for n2 in range(2):
                pt = self.psum[(m * 2 + n2) % 4]
                lastmm = None
                for kc in range(Nd):
                    lastmm = self.mm([tw, conv_toks[kc],
                                      dt_toks[m - 2][1] if m >= 2 and dt_toks[m - 2] else None],
                                     pt[:, :], ws[:, kc * 128:(kc + 1) * 128],
                                     self.B(self.A_XC[kc] + 512 * n2, 512),
                                     start=(kc == 0), stop=(kc == Nd - 1),
                                     inc="pe" if kc == Nd - 1 else None)
                if m < 8:
                    f1 = self.F(self.F_Y[0], 512)
                    ta = k.op("act", lambda e, pt=pt, f1=f1: e.activation(f1, pt[:, :], ACT.Abs),
                              deps=[lastmm, self.t_sp_prev, self.t_fw] + bar, inc="act")
                    tb = k.op("act", lambda e, f1=f1: e.activation(f1, f1, ACT.Exp, scale=-1.0),
                              deps=[ta], inc="act")
                    tc2 = k.op("act", lambda e, f1=f1: e.activation(f1, f1, ACT.Ln, bias=self.fconst[:, 1:2]),
                               deps=[tb, self.t_fc], inc="act")
                    td = k.op("dve", lambda e, m=m, n2=n2, pt=pt, f1=f1: e.scalar_tensor_tensor(
                        self.B(self.A_XCW[m] + 512 * n2, 512), pt[:, :], 0.0, f1, OP.max, OP.add),
                        deps=[tc2, conv_toks[m], self.t_tail[m]] + bar, inc="dve")
                    self.t_sp_prev = td
                    if n2 == 0:
                        dt_half = td
                    else:
                        dt_toks[m] = (dt_half, td)
                else:
                    t_bc = k.op("dve", lambda e, n2=n2, pt=pt: e.tensor_copy(
                        self.BIG[0:32, self.A_BCS + 512 * n2:self.A_BCS + 512 * (n2 + 1)],
                        pt[0:32, :]), deps=[lastmm, t_bc, self.t_fw] + bar, inc="dve")
            self.strip_done(m % 2, lastmm)
        t_w = [None] * Nd
        for i in range(Nd):
            wt = self.B(self.A_W[i], CH)
            t_w[i] = k.op("dve", lambda e, i=i, wt=wt: e.tensor_mul(
                wt, self.B(self.A_XCW[i], CH), self.B(self.A_XC[i], CH)),
                deps=[dt_toks[i][0], dt_toks[i][1], conv_toks[i]] + bar, inc="dve")
        BC = self.BIG[0:32, self.A_BCS:self.A_BCS + CH]
        g = self.t_sc_guard
        for key, slot in ((("B", 1), 2), (("Br", 1), 2), (("C", 1), 3), (("Cr", 1), 3)):
            g[key] = [g.get(key), self.t_stripbuf.get(slot)]
        t_dafree = [None, None]
        t_y3 = [None] * Nd
        t_yfree = None
        t_epi = None
        pB = [self.A_BC[0], self.A_WS2[0]]
        pC = [self.A_BC[1], self.A_WS2[1]]
        psB = [self.psum[4], self.psum[0]]
        psC = [self.psum[5], self.psum[1]]
        for i in range(Nd):
            t_n = None
            for n in range(N_STATE):
                p = n % 2
                sA = self.B(self.A_SC[0 + p], CH)
                sB = self.B(self.A_SC[2], CH)
                sH = self.B(self.A_SC[3], CH)
                sT = sB
                tBn = None
                for n2 in range(2):
                    tmb = self.mm([t_bc, tBn, g.get(("B", p)), g.get(("Br", p)), self.t_const],
                                  psB[p][:, :], self.indbc[:, 128 * n:128 * (n + 1)],
                                  BC[:, 512 * n2:512 * (n2 + 1)], True, True, inc="pe")
                    tBn = k.op("act", lambda e, n2=n2, p=p: e.copy(
                        self.B(pB[p] + 512 * n2, 512), psB[p][:, :]),
                        deps=[tmb, tBn], inc="act")
                tCn = None
                for n2 in range(2):
                    tmc = self.mm([t_bc, tCn, g.get(("C", p)), g.get(("Cr", p))],
                                  psC[p][:, :], self.indbc[:, 2048 + 128 * n:2048 + 128 * (n + 1)],
                                  BC[:, 512 * n2:512 * (n2 + 1)], True, True, inc="pe")
                    tCn = k.op("act", lambda e, n2=n2, p=p: e.copy(
                        self.B(pC[p] + 512 * n2, 512), psC[p][:, :]),
                        deps=[tmc, tCn], inc="act")
                g[("B", p)] = tBn
                g[("C", p)] = tCn
                tda = k.op("act", lambda e, i=i, n=n, sA=sA: e.activation(
                    sA, self.B(self.A_XCW[i], CH), ACT.Exp, scale=-float(n + 1)),
                    deps=[dt_toks[i][0], dt_toks[i][1], t_dafree[p]], inc="act")
                tdb = k.op("dve", lambda e, sB=sB, i=i, p=p: e.tensor_mul(
                    sB, self.B(self.A_W[i], CH), self.B(pB[p], CH)),
                    deps=[t_w[i], tBn, t_n], inc="dve")
                g[("Br", p)] = tdb
                init = 0.0 if c == 0 else self.hcar[:, i * N_STATE + n:i * N_STATE + n + 1]
                dd = [tda, tdb, self.t_mul_prev]
                if c == 1:
                    dd.append(self.t_carry.get((b, i, n)))
                tsc = k.op("dve", lambda e, sA=sA, sB=sB, sH=sH, init=init: e.tensor_tensor_scan(
                    sH, sA, sB, init, OP.mult, OP.add), deps=dd, inc="dve")
                if c == 0:
                    tcar = k.op("dve", lambda e, sH=sH, i=i, n=n: e.tensor_copy(
                        self.hcar[:, i * N_STATE + n:i * N_STATE + n + 1], sH[:, CH - 1:CH]),
                        deps=[tsc], inc="dve")
                    self.t_carry[(b, i, n)] = tcar
                tmul = k.op("pool", lambda e, sH=sH, sA=sA, p=p: e.tensor_mul(
                    sA, sH, self.B(pC[p], CH)), deps=[tsc, tCn], inc="pool")
                g[("Cr", p)] = tmul
                self.t_mul_prev = tmul
                yap = self.F(self.F_Y[0], CH)
                if n == 0:
                    t_n = k.op("dve", lambda e, sA=sA, yap=yap: e.tensor_copy(yap, sA),
                               deps=[tmul, t_yfree], inc="dve")
                elif n % 2 == 0:
                    t_n = k.op("pool", lambda e, sA=sA, yap=yap: e.tensor_add(yap, yap, sA),
                               deps=[tmul, t_n], inc="pool")
                else:
                    t_n = k.op("dve", lambda e, sA=sA, yap=yap: e.tensor_add(yap, yap, sA),
                               deps=[tmul, t_n], inc="dve")
                t_dafree[p] = t_n
            ty2 = k.op("dve", lambda e, i=i: e.scalar_tensor_tensor(
                self.F(self.F_Y[1], CH), self.B(self.A_XC[i], CH), self.d_par[:, i:i + 1],
                self.F(self.F_Y[0], CH), OP.mult, OP.add),
                deps=[t_n, conv_toks[i], t_epi], inc="dve")
            t_yfree = ty2
            ty3 = k.op("dve", lambda e, i=i: e.tensor_mul(
                self.B(self.A_XC[i], CH), self.F(self.F_Y[1], CH), self.B(self.A_Z[i], CH)),
                deps=[ty2, evac[(8 + i, 0)], evac[(8 + i, 1)], t_w[i]], inc="dve")
            t_epi = ty3
            t_y3[i] = ty3
        self.t_stripbuf[2] = [g[("B", 1)], g[("Br", 1)]]
        self.t_stripbuf[3] = [g[("C", 1)], g[("Cr", 1)]]
        t_xs = [[None, None] for _ in range(Nd)]
        for m in range(Nd):
            ws, tw = self.stream_w(t_y3[-1:], "outp", m, 1024, m % 2)
            for n2 in range(2):
                pt = self.psum[(m * 2 + n2) % 4]
                lastmm = None
                for kc in range(Nd):
                    lastmm = self.mm([tw, t_y3[kc],
                                      t_xs[m - 2][n2] if m >= 2 else None], pt[:, :],
                                     ws[:, kc * 128:(kc + 1) * 128],
                                     self.B(self.A_XC[kc] + 512 * n2, 512),
                                     start=(kc == 0), stop=(kc == Nd - 1),
                                     inc="pe" if kc == Nd - 1 else None)
                te = k.op("dve", lambda e, m=m, n2=n2, pt=pt, win=win: e.tensor_add(
                    self.B(self.A_Z[m] + 512 * n2, 512), pt[:, :],
                    self.X[m][:, win + 512 * n2:win + 512 * (n2 + 1)]),
                    deps=[lastmm, t_y3[m]], inc="dve")
                t_xs[m][n2] = te
            self.strip_done(m % 2, lastmm)
        for n2 in range(2):
            xs_tiles = [self.B(self.A_Z[i] + 512 * n2, 512) for i in range(Nd)]
            xo_tiles = [self.X[i][:, win + 512 * n2:win + 512 * (n2 + 1)] for i in range(Nd)]
            self.emit_layernorm([t_xs[i][n2] for i in range(Nd)], xs_tiles, xo_tiles)
        self.bar = k.barrier_tokens()

    def emit_ffn0(self, b, c2):
        k = self.k
        win = b * S + c2 * FCH
        bar = self.bar
        A_H = [self.R0 + 512 * i for i in range(32)]
        A_QX = [self.A_Z[0] + 512 * i for i in range(Nd)]
        xw = [self.X[i][:, win:win + FCH] for i in range(Nd)]
        t_s, t_c = self.emit_col_scales(bar, xw, Nd, save_c2=True)
        qx = [self.B(A_QX[i], FCH) for i in range(Nd)]
        tq = self.emit_quant(bar + [t_s], xw, qx, Nd, t_s)
        t_h = [None] * 32
        gbuf = [self.A_SG, self.A_SC[0]]
        ubuf = [self.A_Q, self.A_SC[2]]
        t_sgfree = [tq[-1], bar]
        t_qfree = [None, bar]
        for m in range(32):
            mp = m % 2
            wsg, twg = self.stream_strip_bf16(bar + [self.t_scr["d0"]], self.scr["g0"], 0, Nd, m, 0)
            wsu, twu = self.stream_strip_bf16(bar + [self.t_scr["d0"]], self.scr["u0"], 0, Nd, m, 1)
            ptg, ptu = self.psum[m % 2], self.psum[2 + m % 2]
            lg = lu = None
            for kc in range(Nd):
                lg = self.mm([twg, tq[kc], t_h[m - 2] if m >= 2 else None],
                             ptg[:, 0:FCH], wsg[:, kc * 128:(kc + 1) * 128],
                             qx[kc], start=(kc == 0), stop=(kc == Nd - 1),
                             inc="pe" if kc == Nd - 1 else None)
            for kc in range(Nd):
                lu = self.mm([twu, tq[kc]], ptu[:, 0:FCH], wsu[:, kc * 128:(kc + 1) * 128],
                             qx[kc], start=(kc == 0), stop=(kc == Nd - 1),
                             inc="pe" if kc == Nd - 1 else None)
            tr = k.op("act", lambda e, ptg=ptg, mp=mp: e.activation(self.B(gbuf[mp], FCH),
                                                                    ptg[:, 0:FCH], ACT.Relu),
                      deps=[lg, t_sgfree[mp]], inc="act")
            tu = k.op("act", lambda e, ptu=ptu, mp=mp: e.copy(self.B(ubuf[mp], FCH),
                                                              ptu[:, 0:FCH]),
                      deps=[lu, t_qfree[mp]], inc="act")
            tr2 = k.op("dve", lambda e, mp=mp: e.tensor_mul(self.B(gbuf[mp], FCH),
                                                            self.B(gbuf[mp], FCH),
                                                            self.B(gbuf[mp], FCH)),
                       deps=[tr], inc="dve")
            th = k.op("dve", lambda e, m=m, mp=mp: e.tensor_mul(self.B(A_H[m], FCH),
                                                                self.B(gbuf[mp], FCH),
                                                                self.B(ubuf[mp], FCH)),
                      deps=[tr2, tu] + bar, inc="dve")
            t_sgfree[mp] = th
            t_qfree[mp] = th
            t_h[m] = th
            self.strip_done(0, lg)
            self.strip_done(1, lu)
        ht = [self.B(A_H[m], FCH) for m in range(32)]
        if globals().get('DEV_TAP3') and b == 0 and c2 == 0:
            if not hasattr(self, "dbg_h"):
                self.dbg_h = self.nc.dram_tensor("dbg", [4, 128, 512], F32, kind="ExternalOutput").ap()
                self.dbgf = self.nc.alloc_sbuf_tensor("dbgf", [128, 512], F32).ap()
            self.t_dbg = getattr(self, "t_dbg", None)
            for di, col in enumerate((A_H[0], A_H[4], A_QX[0], A_QX[7])):
                tcv = k.op("dve", lambda e, col=col: e.tensor_copy(self.dbgf, self.B(col, 512)),
                           deps=[t_h[31], tq[-1], self.t_dbg], inc="dve")
                self.t_dbg = self.dma([tcv], self.dbg_h[di], self.dbgf, sem="dm")
            t_h = [self.t_dbg if t is t_h[0] else t for t in t_h]
            t_h[0] = self.t_dbg
        t_s2, t_c2 = self.emit_col_scales(t_h, ht, 32)
        cx = self.F(self.F_SC[2], FCH)
        ch_ = self.F(self.F_SC[1], FCH)
        ct = self.F(self.F_Y[0], FCH)
        t_ct = k.op("dve", lambda e: e.tensor_mul(ct, cx, cx),
                    deps=[t_c, t_c2, list(self.t_dn_guard)], inc="dve")
        t_ct = k.op("dve", lambda e: e.tensor_mul(ct, ct, cx), deps=[t_ct], inc="dve")
        t_ct = k.op("dve", lambda e: e.tensor_mul(ct, ct, ch_), deps=[t_ct], inc="dve")
        for col, p in ((0, 2), (1, 1), (2, 1)):
            for _ in range(p):
                t_ct = k.op("dve", lambda e, col=col: e.tensor_scalar(
                    ct, ct, self.F(self.F_ROW + 20 + col, 1), None, OP.mult),
                    deps=[t_ct], inc="dve")
        dn_guard = [[self.t_dn_guard[m], list(self.t_ln_ps), bar] for m in range(Nd)]
        last_mm = [None] * Nd
        tq2 = []
        for kc in range(32):
            tq2 += self.emit_quant([t_h[kc], t_s2], ht[kc:kc + 1], ht[kc:kc + 1], 1, t_s2)
            wsd, twd = self.stream_w([tq2[kc]], "d0", kc, 1024, 2 + kc % 2)
            for m in range(Nd):
                last_mm[m] = self.mm(
                    [twd, tq2[kc], dn_guard[m] if kc == 0 else None,
                     (t_h[30 + m % 2] if kc == 0 and m < 4 else None)],
                    self.psum[m][:, 0:FCH],
                    wsd[:, m * 128:(m + 1) * 128], ht[kc],
                    start=(kc == 0), stop=(kc == 31),
                    inc="pe" if m == Nd - 1 else None)
            self.strip_done(2 + kc % 2, last_mm[Nd - 1])
        t_xs = [None] * Nd
        for m in range(Nd):
            tmu = k.op("dve", lambda e, m=m: e.tensor_mul(self.F(self.F_Y[1], FCH),
                                                          self.psum[m][:, 0:FCH], ct),
                       deps=[last_mm[Nd - 1], t_ct, t_xs[m - 1] if m else None], inc="dve")
            txs = k.op("dve", lambda e, m=m, win=win: e.tensor_add(
                self.B(A_QX[m], FCH), self.F(self.F_Y[1], FCH), self.X[m][:, win:win + FCH]),
                deps=[tmu, tq[m]], inc="dve")
            t_xs[m] = txs
        self.t_dn_guard = [t_xs[m] for m in range(Nd)]
        if globals().get('DEV_TAP2') and b == 0 and c2 == 0:
            if not hasattr(self, "dbg_h"):
                self.dbg_h = self.nc.dram_tensor("dbg", [4, 128, 512], F32, kind="ExternalOutput").ap()
                self.dbgf = self.nc.alloc_sbuf_tensor("dbgf", [128, 512], F32).ap()
            self.t_dbg = getattr(self, "t_dbg", None)
            taps = [(self.B(A_QX[0], FCH), t_xs[0]), (self.B(A_H[0], FCH), t_xs[0]),
                    (ct, t_xs[0]), (self.B(A_H[31], FCH), t_xs[0])]
            for di, (col, dtok) in enumerate(taps):
                tcv = k.op("dve", lambda e, col=col: e.tensor_copy(self.dbgf, col),
                           deps=[dtok, self.t_dbg], inc="dve")
                self.t_dbg = self.dma([tcv], self.dbg_h[di], self.dbgf, sem="dm")
        xs_tiles = [self.B(A_QX[i], FCH) for i in range(Nd)]
        xo_tiles = [self.X[i][:, win:win + FCH] for i in range(Nd)]
        self.emit_layernorm(t_xs, xs_tiles, xo_tiles)
        self.bar = k.barrier_tokens()

    def emit_l1_kv(self, b):
        k, D_ = self.k, self.D
        bar = self.bar
        A_CQ = [self.R0 + 2048 * i for i in range(2)]
        A_CKV = [self.R0 + 4096 + 2048 * i for i in range(2)]
        A_KS = [self.A_XC[0] + 2048 * i for i in range(2)]
        self.A_COS = self.A_XC[0] + 4096
        self.A_SIN = self.A_XC[0] + 6144
        A_VT = [self.A_Z[0] + 1040 * i for i in range(2)]
        A_RT = [self.A_Z[0] + 2080 + 512 * i for i in range(2)]
        A_BB = self.A_Z[0] + 3104
        self.A_CQv = A_CQ
        win = b * S
        BQ = self.bias_cols
        tc = None
        for half, nm in ((0, "cosT"), (1, "sinT")):
            for piece in range(S // 512):
                tl = self.dma(list(bar) + [self.t_fw],
                              self.F(self.F_W, 512), D_[nm][:, 512 * piece:512 * (piece + 1)])
                dstc = (self.A_COS if half == 0 else self.A_SIN) + 512 * piece
                tc = k.op("act", lambda e, dstc=dstc: e.copy(self.B(dstc, 512),
                                                             self.F(self.F_W, 512)),
                          deps=[tl], inc="act")
                self.t_fw = tc
        self.t_cossin = tc
        t_pj = {}
        t_pj_seq = []
        for dsti, (dst, wnm, bc0) in enumerate(((A_CQ, "w_q", BQ["b_q"]),
                                                (A_CKV, "w_kv", BQ["b_kv"]))):
            for m in range(2):
                ws, tw = self.stream_w(bar, {"w_q": "q", "w_kv": "kv"}[wnm], m, 1024, m % 2)
                for n4 in range(4):
                    pt = self.psum[n4 % 2]
                    lastmm = None
                    for kc in range(Nd):
                        lastmm = self.mm([tw, t_pj_seq[-2] if len(t_pj_seq) >= 2 else None]
                                         + bar, pt[:, :],
                                         ws[:, kc * 128:(kc + 1) * 128],
                                         self.X[kc][:, win + 512 * n4:win + 512 * (n4 + 1)],
                                         start=(kc == 0), stop=(kc == Nd - 1),
                                         inc="pe" if kc == Nd - 1 else None)
                    t_pj[(dsti, m, n4)] = k.op("act", lambda e, dst=dst, m=m, n4=n4, pt=pt, bc0=bc0:
                                               e.activation(self.B(dst[m] + 512 * n4, 512), pt[:, :],
                                                            ACT.Gelu,
                                                            bias=self.bias_s[:, bc0 + m:bc0 + m + 1]),
                                               deps=[lastmm] + bar, inc="act")
                    t_pj_seq.append(t_pj[(dsti, m, n4)])
                self.strip_done(m % 2, lastmm)
        t_ckv = [t_pj[(1, m, n4)] for m in range(2) for n4 in range(4)]
        self.t_cq = [t_pj[(0, m, n4)] for m in range(2) for n4 in range(4)]
        t_bv = None
        for s4 in range(8):
            tl = self.dma([self.t_bvrow_free if s4 else self.t_fw], self.F(self.F_W, 128)[0:1, :],
                          D_["b_v"][s4:s4 + 1, :])
            self.t_fw = tl
            tmb = self.mm([tl, t_bv], self.psum[5][:, 0:128], self.ones_rf[:, :],
                          self.F(self.F_W, 128)[0:1, :], True, True, inc="pe")
            self.t_fw = tmb
            t_bv = k.op("dve", lambda e, s4=s4: e.tensor_copy(
                self.B(A_BB + 128 * s4, 128), self.psum[5][:, 0:128]), deps=[tmb], inc="dve")
            self.t_bvrow_free = t_bv
        tkn = None
        t_ks = None
        t_kw = []
        for m in range(Nd):
            if m < 7:
                ws, tw = self.stream_w(bar, "kc", m, 256, m % 2)
                for n4 in range(4):
                    pt = self.psum[2]
                    lastmm = None
                    for kc in range(2):
                        lastmm = self.mm([tw, t_ks, t_ckv[kc * 4 + n4]], pt[:, :],
                                         ws[:, kc * 128:(kc + 1) * 128],
                                         self.B(A_CKV[kc] + 512 * n4, 512),
                                         start=(kc == 0), stop=(kc == 1),
                                         inc="pe" if kc == 1 else None)
                    t_ks = k.op("dve", lambda e, m=m, n4=n4, pt=pt: self.badd(
                        e, self.B(A_KS[m % 2] + 512 * n4, 512), pt[:, :],
                        self.bias_s[:, BQ["b_kc"] + m:BQ["b_kc"] + m + 1]),
                        deps=[lastmm, t_kw[m - 2] if m >= 2 else None], inc="dve")
            else:
                ws, tw = self.stream_w(bar, "kr", 0, 1024, 0)
                ws2, tw2 = self.stream_w(bar, "kr_p", 0, 1024, 1)
                for n4 in range(4):
                    pt, pt2 = self.psum[2], self.psum[3]
                    l1 = l2 = None
                    for kc in range(Nd):
                        l1 = self.mm([tw, t_ks], pt[:, :], ws[:, kc * 128:(kc + 1) * 128],
                                     self.X[kc][:, win + 512 * n4:win + 512 * (n4 + 1)],
                                     start=(kc == 0), stop=(kc == Nd - 1),
                                     inc="pe" if kc == Nd - 1 else None)
                    for kc in range(Nd):
                        l2 = self.mm([tw2], pt2[:, :], ws2[:, kc * 128:(kc + 1) * 128],
                                     self.X[kc][:, win + 512 * n4:win + 512 * (n4 + 1)],
                                     start=(kc == 0), stop=(kc == Nd - 1),
                                     inc="pe" if kc == Nd - 1 else None)
                    ta = k.op("dve", lambda e, pt=pt: self.badd(
                        e, self.B(A_RT[0], 512), pt[:, :],
                        self.bias_s[:, BQ["b_kr"]:BQ["b_kr"] + 1]),
                        deps=[l1, self.t_rt_free], inc="dve")
                    tb2 = k.op("dve", lambda e, pt2=pt2: self.badd(
                        e, self.B(A_RT[1], 512), pt2[:, :],
                        self.bias_s[:, BQ["b_kr_p"]:BQ["b_kr_p"] + 1]),
                        deps=[l2], inc="dve")
                    tr1 = k.op("dve", lambda e, n4=n4: e.tensor_mul(
                        self.B(A_KS[1] + 512 * n4, 512), self.B(A_RT[0], 512),
                        self.B(self.A_COS + 512 * n4, 512)),
                        deps=[ta, self.t_cossin, t_kw[m - 2] if m >= 2 else None], inc="dve")
                    tr2 = k.op("dve", lambda e, n4=n4: e.tensor_mul(
                        self.B(A_RT[0], 512), self.B(A_RT[1], 512),
                        self.B(self.A_SIN + 512 * n4, 512)), deps=[tb2, tr1], inc="dve")
                    t_ks = k.op("dve", lambda e, n4=n4: e.tensor_add(
                        self.B(A_KS[1] + 512 * n4, 512), self.B(A_KS[1] + 512 * n4, 512),
                        self.B(A_RT[0], 512)), deps=[tr2], inc="dve")
                    self.t_rt_free = t_ks
            if m < 7:
                self.strip_done(m % 2, t_ks)
            else:
                self.strip_done(0, l1)
                self.strip_done(1, l2)
            if m % 2 == 1:
                t_gel = []
                for half in range(2):
                    tg = k.op("act", lambda e, half=half, A_KS=A_KS: e.activation(
                        self.B(A_KS[half], 2048), self.B(A_KS[half], 2048), ACT.Gelu),
                        deps=[t_ks], inc="act")
                    t_gel.append(tg)
                for half in range(2):
                    strip = m - 1 + half
                    for n4 in range(4):
                        tsq = k.op("act", lambda e, half=half, n4=n4, A_KS=A_KS: e.activation(
                            self.B(self.A_SG, 512), self.B(A_KS[half] + 512 * n4, 512), ACT.Square),
                            deps=[t_gel[half], tkn], inc="act")
                        tq2_ = self.mm([tsq], self.psum[6][0:16, :],
                                       self.blockones[:, strip * 16:(strip + 1) * 16],
                                       self.B(self.A_SG, 512), True, True, inc="pe")
                        tkm = k.op("dve", lambda e: e.reduce_max(self.knmax[:, 0:1],
                                                                 self.psum[6][0:16, :], AX),
                                   deps=[tq2_], inc="dve")
                        if tkn is None and strip == 0 and n4 == 0:
                            tkn = k.op("dve", lambda e: e.tensor_copy(self.knmax[:, 1:2],
                                                                      self.knmax[:, 0:1]),
                                       deps=[tkm], inc="dve")
                        else:
                            tkn = k.op("dve", lambda e: e.tensor_max(self.knmax[:, 1:2],
                                                                     self.knmax[:, 1:2],
                                                                     self.knmax[:, 0:1]),
                                       deps=[tkm, tkn], inc="dve")
                tw1 = self.dma([t_gel[0], tkn], self.scr["kk"][m - 1],
                               self.B(A_KS[0], 2048), sem="dw")
                tw2_ = self.dma([t_gel[1], tkn], self.scr["kk"][m],
                                self.B(A_KS[1], 2048), sem="dw")
                t_kw.append(tw2_)
                t_kw.append(tw2_)
                t_ks = tw2_
        self.t_knmax = tkn
        self.t_kscr = t_kw[-1]
        wsv0, twv0 = self.stream_w(bar, "v", 0, 1024, 0)
        wsv1, twv1 = self.stream_w(bar, "v", 1, 1024, 1)
        t_vprev = [None, None]
        t_evprev = [None, None]
        for tt in range(16):
            vb = A_VT[tt % 2]
            lastmm = [None, None]
            for n2 in range(2):
                pt = self.psum[n2]
                lmm = None
                for kc in range(2):
                    lmm = self.mm([twv0, twv1, t_vprev[tt % 2], t_evprev[n2],
                                   t_ckv[kc * 4 + tt // 4]],
                                  pt[:, :], self.B(A_CKV[kc] + 128 * tt, 128),
                                  (wsv0 if kc == 0 else wsv1)[:, 512 * n2:512 * (n2 + 1)],
                                  start=(kc == 0), stop=(kc == 1),
                                  inc="pe" if kc == 1 else None)
                lastmm[n2] = lmm
            tev1 = k.op("dve", lambda e, vb=vb: e.tensor_add(
                self.BIG[:, vb:vb + 520].rearrange("p (h c) -> p h c", c=65)[:, :, 0:64],
                self.psum[0][:, :].rearrange("p (h c) -> p h c", c=64),
                self.B(A_BB, 512).rearrange("p (h c) -> p h c", c=64)),
                deps=[lastmm[0], t_bv, t_vprev[tt % 2]], inc="dve")
            tev2 = k.op("dve", lambda e, vb=vb: e.tensor_add(
                self.BIG[:, vb + 520:vb + 1040].rearrange("p (h c) -> p h c", c=65)[:, :, 0:64],
                self.psum[1][:, :].rearrange("p (h c) -> p h c", c=64),
                self.B(A_BB + 512, 512).rearrange("p (h c) -> p h c", c=64)),
                deps=[lastmm[1], t_bv, tev1], inc="dve")
            tos = k.op("dve", lambda e, vb=vb: e.memset(
                self.BIG[:, vb:vb + 1040].rearrange("p (h c) -> p h c", c=65)[:, :, 64:65], 1.0),
                deps=[tev2], inc="dve")
            t_evprev = [tev1, tev2]
            tvw = self.dma([tos], self.scr["vv"][tt], self.BIG[:, vb:vb + 1040], sem="dw")
            t_vprev[tt % 2] = tvw
        self.strip_done(0, lastmm[1])
        self.strip_done(1, lastmm[1])
        self.t_vscr = [t_vprev[0], t_vprev[1]]
        self.bar = k.barrier_tokens()

    def emit_l1_chunk(self, b, c2):
        k, D_ = self.k, self.D
        bar = self.bar
        win = b * S + c2 * FCH
        BQ = self.bias_cols
        A_CQ = self.A_CQv
        A_QT = [self.A_W[0] + 512 * i for i in range(Nd)]
        A_AT = [self.A_W[0] + 4096 + 512 * i for i in range(Nd)]
        A_KB = [self.A_SC[0] + 256 * i for i in range(2)]
        A_VB = [self.A_SC[0] + 512 + 260 * i for i in range(2)]
        A_P = [self.A_SC[0] + 1040 + 512 * i for i in range(2)]
        A_R1 = self.A_BC[0]
        A_R2 = self.A_BC[0] + 512
        A_EO = self.A_BC[1]
        t_q = [None] * Nd
        for m in range(7):
            ws, tw = self.stream_w(bar, "qc", m, 256, m % 2)
            pt = self.psum[6]
            lastmm = None
            for kc in range(2):
                lastmm = self.mm([tw, t_q[m - 1] if m else None] + bar, pt[:, :],
                                 ws[:, kc * 128:(kc + 1) * 128],
                                 self.B(A_CQ[kc] + 512 * c2, 512),
                                 start=(kc == 0), stop=(kc == 1), inc="pe" if kc == 1 else None)
            t_q[m] = k.op("dve", lambda e, m=m, pt=pt: self.badd(
                e, self.B(A_QT[m], 512), pt[:, :],
                self.bias_s[:, BQ["b_qc"] + m:BQ["b_qc"] + m + 1]),
                deps=[lastmm] + bar, inc="dve")
            self.strip_done(m % 2, lastmm)
        ws, tw = self.stream_w(bar, "qr", 0, 1024, 0)
        ws2, tw2 = self.stream_w(bar, "qr_p", 0, 1024, 1)
        pt, pt2 = self.psum[6], self.psum[7]
        t_pre = None
        if globals().get('DEV_TAP') and b == 0 and c2 == 0:
            pre_dbg = self.nc.alloc_sbuf_tensor("pre_dbg", [128, 512], F32).ap()
            pre_h = self.nc.dram_tensor("pre_h", [128, 512], F32, kind="ExternalOutput").ap()
            tpv = k.op("dve", lambda e, pt=pt: e.tensor_copy(pre_dbg[:, :], pt[:, :]),
                       deps=[t_q[6]], inc="dve")
            t_pre = self.dma([tpv], pre_h[:, :], pre_dbg[:, :], sem="dm")
        l1 = l2 = None
        for kc in range(Nd):
            l1 = self.mm([tw, t_q[6], t_pre], pt[:, :], ws[:, kc * 128:(kc + 1) * 128],
                         self.X[kc][:, win:win + 512],
                         start=(kc == 0), stop=(kc == Nd - 1), inc="pe" if kc == Nd - 1 else None)
        for kc in range(Nd):
            l2 = self.mm([tw2], pt2[:, :], ws2[:, kc * 128:(kc + 1) * 128],
                         self.X[kc][:, win:win + 512],
                         start=(kc == 0), stop=(kc == Nd - 1), inc="pe" if kc == Nd - 1 else None)
        self.strip_done(0, l1)
        self.strip_done(1, l2)
        if globals().get('DEV_TAP') and b == 0 and c2 == 0:
            qr_dbg = self.nc.alloc_sbuf_tensor("qr_dbg", [128, 512], BF16).ap()
            ta0 = k.op("dve", lambda e, pt=pt: self.badd(e, qr_dbg[:, :], pt[:, :],
                                                  self.bias_s[:, BQ["b_qr"]:BQ["b_qr"] + 1]),
                       deps=[l1] + bar, inc="dve")
            ta = k.op("dve", lambda e: e.tensor_copy(self.B(A_R1, 512), qr_dbg[:, :]),
                      deps=[ta0], inc="dve")
        else:
            ta = k.op("dve", lambda e, pt=pt: self.badd(e, self.B(A_R1, 512), pt[:, :],
                                                 self.bias_s[:, BQ["b_qr"]:BQ["b_qr"] + 1]),
                      deps=[l1] + bar, inc="dve")
        tb2 = k.op("dve", lambda e, pt2=pt2: self.badd(e, self.B(A_R2, 512), pt2[:, :],
                                              self.bias_s[:, BQ["b_qr_p"]:BQ["b_qr_p"] + 1]),
                   deps=[l2] + bar, inc="dve")
        if globals().get('DEV_TAP') and b == 0 and c2 == 0:
            if not hasattr(self, "dbg_h"):
                self.dbg_h = self.nc.dram_tensor("dbg", [4, 128, 512], F32, kind="ExternalOutput").ap()
                self.dbgf = self.nc.alloc_sbuf_tensor("dbgf", [128, 512], F32).ap()
            self.t_dbg = getattr(self, "t_dbg", None)
            taps = ((self.B(A_R1, 512), ta, "copy"), (pt[:, :], ta, "copy"),
                    (pt[:, :], ta, "ts"), (pt[:, :], ta, "ts_imm"))
            for di, (src_ap, dtok, mode) in enumerate(taps):
                if mode == "copy":
                    tcv = k.op("dve", lambda e, src_ap=src_ap: e.tensor_copy(self.dbgf, src_ap),
                               deps=[dtok, self.t_dbg], inc="dve")
                elif mode == "ts":
                    tcv = k.op("dve", lambda e, src_ap=src_ap: e.tensor_scalar(
                        self.dbgf, src_ap, self.bias_s[:, 9:10], None, OP.add),
                        deps=[dtok, self.t_dbg], inc="dve")
                else:
                    tcv = k.op("dve", lambda e, src_ap=src_ap: e.tensor_scalar(
                        self.dbgf, src_ap, 0.25, None, OP.add),
                        deps=[dtok, self.t_dbg], inc="dve")
                self.t_dbg = self.dma([tcv], self.dbg_h[di], self.dbgf, sem="dm")
        tr1 = k.op("dve", lambda e: e.tensor_mul(self.B(A_QT[7], 512), self.B(A_R1, 512),
                                                 self.B(self.A_COS + 512 * c2, 512)),
                   deps=[ta, self.t_cossin, getattr(self, "t_dbg", None)] + bar, inc="dve")
        tr2 = k.op("dve", lambda e: e.tensor_mul(self.B(A_R1, 512), self.B(A_R2, 512),
                                                 self.B(self.A_SIN + 512 * c2, 512)),
                   deps=[tb2, tr1], inc="dve")
        t_q[7] = k.op("dve", lambda e: e.tensor_add(self.B(A_QT[7], 512), self.B(A_QT[7], 512),
                                                    self.B(A_R1, 512)), deps=[tr2], inc="dve")
        if globals().get('DEV_TAP_OLD') and b == 0 and c2 == 0:
            if not hasattr(self, "dbg_h"):
                self.dbg_h = self.nc.dram_tensor("dbg", [4, 128, 512], F32, kind="ExternalOutput").ap()
                dbf = self.nc.alloc_sbuf_tensor("dbgf", [128, 512], F32).ap()
                self.dbgf = dbf
            for di, col in enumerate((A_QT[7], A_R1, A_R2, self.A_COS + 512 * c2)):
                tcv = k.op("dve", lambda e, col=col: e.tensor_copy(self.dbgf, self.B(col, 512)),
                           deps=[t_q[7], self.t_dbg], inc="dve")
                self.t_dbg = self.dma([tcv], self.dbg_h[di], self.dbgf, sem="dm")
        t_gel = [None] * Nd
        for m in range(Nd):
            t_gel[m] = k.op("act", lambda e, m=m: e.activation(self.B(A_QT[m], 512),
                                                               self.B(A_QT[m], 512), ACT.Gelu),
                            deps=[t_q[m], getattr(self, "t_dbg", None)], inc="act")
        tq2_ = None
        tsq = None
        for m in range(Nd):
            tsq = k.op("act", lambda e, m=m: e.activation(self.B(self.A_SG, 512),
                                                          self.B(A_QT[m], 512), ACT.Square),
                       deps=[t_gel[m], tsq, tq2_], inc="act")
            tq2_ = self.mm([tsq], self.psum[6][0:16, :], self.blockones[:, m * 16:(m + 1) * 16],
                           self.B(self.A_SG, 512), start=(m == 0), stop=(m == Nd - 1), inc="pe")
        tqm = k.op("dve", lambda e: e.reduce_max(self.qnmax[:, 0:1], self.psum[6][0:16, :], AX),
                   deps=[tq2_], inc="dve")
        tm1 = k.op("dve", lambda e: e.tensor_add(self.mrow[:, 0:1], self.qnmax[:, 0:1],
                                                 self.knmax[:, 1:2]),
                   deps=[tqm, self.t_knmax], inc="dve")
        tm2 = k.op("dve", lambda e: e.tensor_scalar(self.mrow[:, 0:1], self.mrow[:, 0:1],
                                                    -1.0 / 16.0, None, OP.mult),
                   deps=[tm1], inc="dve")
        tdm = self.dma([tm2, self.t_negm_mm], self.rowt[:, :], self.mrow[:, :],
                       sem="dm", allow_nc=True)
        tnm_mm = self.mm([tdm, self.t_ones], self.psum[7][:, 0:16], self.ones_rf[:, :],
                         self.rowt[:, :], True, True, inc="pe")
        self.t_negm_mm = tnm_mm
        t_negm = k.op("dve", lambda e: e.tensor_copy(self.negm[:, :], self.psum[7][:, 0:16]),
                      deps=[tnm_mm], inc="dve")
        jmax = 4 * c2 + 3
        t_at = [None] * Nd
        t_rc = None
        t_eo = None
        t_hg_prev = None
        for hg in range(4):
            t_kb = [t_hg_prev, t_hg_prev]
            t_vb = [t_hg_prev, t_hg_prev]
            t_av = [None] * 4
            t_pfree = [None, None]
            pend = None
            av_map = {}

            def flush(pend):
                if pend is None:
                    return
                (pA, vb_, hh_, pbuf_, ji_, j_, tvl_, tE_) = pend
                tAV = self.k.op("pe", lambda e, pA=pA, vb_=vb_, hh_=hh_, pbuf_=pbuf_,
                                ji_=ji_, j_=j_: e.matmul(
                                    pA[0:65, :], vb_[:, 65 * hh_:65 * hh_ + 65],
                                    pbuf_, start=(ji_ == 0), stop=(j_ == 0),
                                    skip_group_check=True),
                                deps=[tvl_, tE_, t_av[hh_]], inc="pe")
                t_av[hh_] = tAV
                t_pfree[(ji_ * 4 + hh_) % 2] = tAV
                av_map[(ji_, hh_)] = tAV

            for ji, j in enumerate(range(jmax, -1, -1)):
                off = max(0, 128 * j - 512 * c2)
                kb = self.B(A_KB[ji % 2], 256)
                vb = self.B(A_VB[ji % 2], 260)
                diag = 128 * j >= 512 * c2
                tkl = self.dma([t_kb[ji % 2], self.t_kscr] + bar,
                               kb.rearrange("p (a j) -> p a j", j=128),
                               self.scr["kk"][2 * hg:2 * hg + 2, :, 128 * j:128 * (j + 1)]
                               .rearrange("a p j -> p a j"), sem="dk%d" % (ji % 2), eng="sp")
                tvl = self.dma([av_map.get((ji - 2, 3), t_hg_prev),
                                self.t_vscr[0], self.t_vscr[1]] + bar, vb,
                               self.scr["vv"][j][:, 260 * hg:260 * (hg + 1)],
                               sem="dv%d" % (ji % 2), eng="sp")
                tS = None
                for hh in range(4):
                    h = 4 * hg + hh
                    pS = self.psum[4 + ((ji * 4 + hh) % 2)]
                    lhsK = kb[64 * (hh % 2):64 * (hh % 2) + 64,
                              128 * (hh // 2):128 * (hh // 2) + 128]
                    rhsQ = self.B(A_QT[h // 2], 512)[64 * (h % 2):64 * (h % 2) + 64, :]
                    tS = self.mm([tkl, t_gel[h // 2], t_pfree[(ji * 4 + hh) % 2]],
                                 pS[:, :], lhsK, rhsQ, True, True, inc="pe")
                    pbuf = self.B(A_P[(ji * 4 + hh) % 2], 512)
                    tE = k.op("act", lambda e, pS=pS, pbuf=pbuf, h=h: e.activation(
                        pbuf, pS[:, :], ACT.Exp,
                        bias=self.negm[:, h:h + 1], scale=0.125),
                        deps=[tS, t_negm], inc="act")
                    if diag:
                        tE = k.op("dve", lambda e, pbuf=pbuf, off=off: e.tensor_mul(
                            pbuf[:, off:off + 128], pbuf[:, off:off + 128], self.tri[:, :]),
                            deps=[tE, self.t_const], inc="dve")
                    if off > 0:
                        tE = k.op("dve", lambda e, pbuf=pbuf, off=off: e.memset(
                            pbuf[:, 0:off], 0.0), deps=[tE], inc="dve")
                    flush(pend)
                    pend = (self.psum[hh], vb, hh, pbuf, ji, j, tvl, tE)
                t_kb[ji % 2] = tS
            flush(pend)
            pend = None
            t_hg_prev = t_av[3]
            for hh in range(4):
                h = 4 * hg + hh
                pA = self.psum[hh]
                def _recip(e, pA=pA):
                    with self.nc.allow_low_precision(reason="attn denom recip to bf16"):
                        return e.reciprocal(self.trcp[64:65, :], pA[64:65, :])
                trc = k.op("dve", _recip, deps=[t_av[hh], self.t_trcp0, t_rc], inc="dve")
                tbc = self.mm([trc], self.psum[6][0:64, :], self.e64[:, :], self.trcp[0:65, :],
                              True, True, inc="pe")
                trb = k.op("dve", lambda e: e.tensor_copy(self.B(self.A_Q, 512)[0:64, :],
                                                          self.psum[6][0:64, :]),
                           deps=[tbc], inc="dve")
                if h % 2 == 0:
                    t_at[h // 2] = k.op("dve", lambda e, pA=pA, h=h: e.tensor_mul(
                        self.B(A_AT[h // 2], 512)[0:64, :], pA[0:64, :],
                        self.B(self.A_Q, 512)[0:64, :]),
                        deps=[trb, t_at[h // 2]] + bar, inc="dve")
                    t_rc = t_at[h // 2]
                else:
                    teo = k.op("dve", lambda e, pA=pA: e.tensor_mul(
                        self.B(A_EO, 512)[0:64, :], pA[0:64, :], self.B(self.A_Q, 512)[0:64, :]),
                        deps=[trb, t_eo] + bar, inc="dve")
                    tdo = self.dma([teo, t_at[h // 2]],
                                   self.BIG[64:128, A_AT[h // 2]:A_AT[h // 2] + 512],
                                   self.B(A_EO, 512)[0:64, :], sem="dm")
                    t_at[h // 2] = tdo
                    t_eo = tdo
                    t_rc = teo
        t_xs = [None] * Nd
        for m in range(Nd):
            ws, tw = self.stream_w(bar + [t_rc], "o", m, 1024, m % 2)
            pt = self.psum[4 + m % 2]
            lastmm = None
            for kc in range(Nd):
                lastmm = self.mm([tw, t_at[kc], t_xs[m - 2] if m >= 2 else None],
                                 pt[:, 0:512], ws[:, kc * 128:(kc + 1) * 128],
                                 self.B(A_AT[kc], 512), start=(kc == 0), stop=(kc == Nd - 1),
                                 inc="pe" if kc == Nd - 1 else None)
            tx1 = k.op("dve", lambda e, m=m, pt=pt, win=win: e.tensor_add(
                self.B(A_QT[m], 512), pt[:, 0:512], self.X[m][:, win:win + 512]),
                deps=[lastmm, t_gel[m], t_xs[m - 1] if m else None], inc="dve")
            txs = k.op("dve", lambda e, m=m: self.badd(
                e, self.B(A_QT[m], 512), self.B(A_QT[m], 512),
                self.bias_s[:, BQ["b_o"] + m:BQ["b_o"] + m + 1]),
                deps=[tx1], inc="dve")
            t_xs[m] = txs
            self.strip_done(m % 2, lastmm)
        xs_tiles = [self.B(A_QT[i], 512) for i in range(Nd)]
        xo_tiles = [self.X[i][:, win:win + 512] for i in range(Nd)]
        self.emit_layernorm(t_xs, xs_tiles, xo_tiles)
        self.bar = k.barrier_tokens()

    def emit_ffn1(self, b, c2):
        k = self.k
        win = b * S + c2 * FCH
        t0 = c2 * FCH
        bar = self.bar
        A_H = [self.R0 + 512 * i for i in range(24)]
        A_XS = [self.A_Z[0] + 512 * i for i in range(Nd)]
        t_h = [None] * 24
        gbuf = [self.A_SG, self.A_SC[0]]
        ubuf = [self.A_Q, self.A_SC[2]]
        t_sgfree = [bar, bar]
        t_qfree = [bar, bar]
        for m in range(24):
            mp = m % 2
            wsg, twg = self.stream_strip_bf16(bar + [self.t_scr["d1"]], self.scr["g1"], 0, Nd, m, 0)
            wsu, twu = self.stream_strip_bf16(bar + [self.t_scr["d1"]], self.scr["u1"], 0, Nd, m, 1)
            ptg, ptu = self.psum[m % 2], self.psum[2 + m % 2]
            lg = lu = None
            for kc in range(Nd):
                lg = self.mm([twg, t_h[m - 2] if m >= 2 else None] + bar, ptg[:, 0:FCH],
                             wsg[:, kc * 128:(kc + 1) * 128],
                             self.X[kc][:, win:win + FCH], start=(kc == 0), stop=(kc == Nd - 1),
                             inc="pe" if kc == Nd - 1 else None)
            for kc in range(Nd):
                lu = self.mm([twu] + bar, ptu[:, 0:FCH], wsu[:, kc * 128:(kc + 1) * 128],
                             self.X[kc][:, win:win + FCH], start=(kc == 0), stop=(kc == Nd - 1),
                             inc="pe" if kc == Nd - 1 else None)
            tg = k.op("act", lambda e, ptg=ptg, mp=mp: e.activation(
                self.B(gbuf[mp], FCH), ptg[:, 0:FCH], ACT.Silu, scale=1.0 / 127.0),
                deps=[lg, t_sgfree[mp]] + bar, inc="act")
            tu = k.op("act", lambda e, ptu=ptu, mp=mp: e.activation(
                self.B(ubuf[mp], FCH), ptu[:, 0:FCH], ACT.Copy, scale=1.0 / 127.0),
                deps=[lu, t_qfree[mp]] + bar, inc="act")
            th = k.op("dve", lambda e, m=m, mp=mp: e.tensor_mul(
                self.B(A_H[m], FCH), self.B(gbuf[mp], FCH), self.B(ubuf[mp], FCH)),
                deps=[tg, tu] + bar, inc="dve")
            t_sgfree[mp] = th
            t_qfree[mp] = th
            t_h[m] = th
            self.strip_done(0, lg)
            self.strip_done(1, lu)
        ht = [self.B(A_H[m], FCH) for m in range(24)]
        dn_guard = [[self.t_dn_guard[m], list(self.t_ln_ps), bar] for m in range(Nd)]
        last_mm = [None] * Nd
        for kc in range(24):
            wsd, twd = self.stream_w([t_h[kc]], "d1", kc, 1024, 2 + kc % 2)
            for m in range(Nd):
                last_mm[m] = self.mm(
                    [twd, t_h[kc], dn_guard[m] if kc == 0 else None,
                     (t_h[22 + m % 2] if kc == 0 and m < 4 else None)],
                    self.psum[m][:, 0:FCH],
                    wsd[:, m * 128:(m + 1) * 128], ht[kc],
                    start=(kc == 0), stop=(kc == 23),
                    inc="pe" if m == Nd - 1 else None)
            self.strip_done(2 + kc % 2, last_mm[Nd - 1])
        t_xs = [None] * Nd
        for m in range(Nd):
            txs = k.op("dve", lambda e, m=m, win=win: e.scalar_tensor_tensor(
                self.B(A_XS[m], FCH), self.psum[m][:, 0:FCH], 1.0 / 127.0,
                self.X[m][:, win:win + FCH], OP.mult, OP.add),
                deps=[last_mm[Nd - 1], t_xs[m - 1] if m else None] + bar, inc="dve")
            t_xs[m] = txs
        self.t_dn_guard = [t_xs[m] for m in range(Nd)]
        xs_tiles = [self.B(A_XS[i], FCH) for i in range(Nd)]
        xo_tiles = [self.X[i][:, win:win + FCH] for i in range(Nd)]
        self.emit_layernorm(t_xs, xs_tiles, xo_tiles, f32_dma=(b, t0))
        self.bar = k.barrier_tokens()

    def finalize(self):
        k = self.k
        final_st = k.cnt["dst"]
        k.op("sp", lambda e: None, deps=[("dst", final_st)])
        ctx = contextlib.ExitStack()
        for s in k.semnames:
            k.sem[s] = ctx.enter_context(self.nc.semaphore(s))
        with self.nc.Block() as block:
            @block.tensor
            def _(e):
                for fn in k.q["pe"]:
                    fn(e)

            @block.vector
            def _(e):
                for fn in k.q["dve"]:
                    fn(e)

            @block.scalar
            def _(e):
                for fn in k.q["act"]:
                    fn(e)

            @block.gpsimd
            def _(e):
                for fn in k.q["pool"]:
                    fn(e)

            @block.sync
            def _(e):
                for fn in k.q["sp"]:
                    fn(e)
        ctx.close()
        return self.nc


def _emit_x_dump(p):
    k = p.k
    t = p.bar[-1] if p.bar else None
    tprev = None
    for b in range(B_CORE):
        for i in range(Nd):
            for w4 in range(4):
                tc = k.op("act", lambda e, b=b, i=i, w4=w4: e.copy(
                    p.F(p.F_Y[0], 512), p.X[i][:, b * S + 512 * w4:b * S + 512 * (w4 + 1)]),
                    deps=p.bar + [tprev], inc="act")
                tprev = p.dma([tc], p.out_h[b, 128 * i:128 * (i + 1), 512 * w4:512 * (w4 + 1)],
                              p.F(p.F_Y[0], 512), sem="dst")
    p.t_out_last = tprev


def build_program():
    p = Prog()
    p.t_rt_free = None
    p.t_bvrow_free = None
    p.t_sp_prev = None
    p.t_ln_ps = ()
    p.t_ln_app = None
    p.t_dn_guard = [None] * 8
    p.t_mul_prev = None
    p.t_negm_mm = None
    p.t_dbg = None
    p.emit_init()
    p.emit_weight_prep()
    for b in range(B_CORE):
        for c in range(2):
            p.emit_l0_chunk(b, c)
        if globals().get('DEV_STAGE') == "l0":
            continue
        for c2 in range(4):
            p.emit_ffn0(b, c2)
    if globals().get('DEV_STAGE') in ("l0", "ffn0"):
        _emit_x_dump(p)
        p.finalize()
        return p
    for b in range(B_CORE):
        p.emit_l1_kv(b)
        if globals().get('DEV_STAGE') == "kv":
            break
        for c2 in range(4):
            p.emit_l1_chunk(b, c2)
        if globals().get('DEV_STAGE') == "attn" and b == 0:
            break
        for c2 in range(4):
            p.emit_ffn1(b, c2)
    if globals().get('DEV_STAGE') in ("kv", "attn"):
        _emit_x_dump(p)
    p.finalize()
    return p


def kernel(**inputs):
    p = build_program()
    in_names = list(p.D.keys())
    in_maps = []
    for core in range(NCORES):
        hm = host_prep(inputs, core)
        in_maps.append({"in_" + n: np.ascontiguousarray(hm[n]) for n in in_names})
    res = run_bass_kernel_spmd(p.nc, in_maps, core_ids=list(range(NCORES)))
    outs = []
    for core in range(NCORES):
        o = np.asarray(res.results[core]["out"])
        outs.append(o.transpose(0, 2, 1))
    return np.ascontiguousarray(np.concatenate(outs, axis=0)).astype(np.float32)


# version 14 (final)
# speedup vs baseline: 1.0450x; 1.0450x over previous
import contextlib

import numpy as np
import concourse.bass as bass
import concourse.mybir as mybir
from concourse.bass_utils import run_bass_kernel_spmd
from concourse.bass import broadcast_tensor_aps

F32 = mybir.dt.float32
BF16 = mybir.dt.bfloat16
I32 = mybir.dt.int32
AX = mybir.AxisListType.X
OP = mybir.AluOpType
ACT = mybir.ActivationFunctionType

B_FULL, S, D = 16, 2048, 1024
NCORES = 8
B_CORE = 2
TOK = B_CORE * S
Nd = D // 128
N_STATE = 16
K_CONV = 4
I0, I1 = 4096, 3072
H, DC, DR, HD = 16, 256, 128, 64
EPS_Q = 1e-05
LN_EPS = 1e-05
CH = 1024
FCH = 512
NBIG = 45312
NFP = 5600


class KB:
    ENGS = ("pe", "dve", "act", "pool", "sp")

    def __init__(self, nc):
        self.nc = nc
        self.q = {e: [] for e in self.ENGS}
        self.semnames = ["pe", "dve", "act", "pool", "sp", "dfw", "d0", "d1", "d2",
                         "d3", "dk0", "dk1", "dv0", "dv1", "dm", "dw", "dst"]
        self.sem = {}
        self.cnt = {n: 0 for n in self.semnames}
        self.waited = {e: {} for e in self.ENGS}

    def op(self, eng, fn, deps=(), inc=None, dinc=None):
        flat = []
        stack = list(deps)
        while stack:
            d = stack.pop()
            if d is None:
                continue
            if isinstance(d, list):
                stack.extend(d)
            else:
                flat.append(d)
        deps = flat
        if eng == "sp":
            dd = {}
            for s, v in deps:
                dd[s] = max(dd.get(s, 0), v)
            deps2 = tuple(dd.items())
        else:
            deps2 = tuple((s, v) for (s, v) in deps
                          if v > 0 and self.waited[eng].get(s, 0) < v)
        for s, v in deps2:
            self.waited[eng][s] = v
        tok, sname, amt = None, None, 0
        if inc is not None:
            self.cnt[inc] += 1
            tok = (inc, self.cnt[inc])
            sname, amt = inc, 1
        elif dinc is not None:
            if self.cnt[dinc] > 0:
                deps2 = tuple(dict(list(deps2) + [(dinc, self.cnt[dinc])]).items())
            self.cnt[dinc] += 16
            tok = (dinc, self.cnt[dinc])
            sname, amt = dinc, 16
        sem = self.sem

        def cl(e, fn=fn, deps2=deps2, sname=sname, amt=amt):
            for s, v in deps2:
                e.wait_ge(sem[s], v)
            r = fn(e)
            if sname is not None:
                assert r is not None
                r.then_inc(sem[sname], amt)
        self.q[eng].append(cl)
        return tok

    def barrier_tokens(self):
        return [(e, self.cnt[e]) for e in ("pe", "dve", "act", "pool")] + \
               [("dfw", self.cnt["dfw"]), ("dw", self.cnt["dw"])]


_CS_CACHE = {}


def _rope_tables():
    if "cs" not in _CS_CACHE:
        import jax
        import jax.numpy as jnp
        with jax.default_device(jax.devices("cpu")[0]):
            inv = 1.0 / 10000.0 ** (jnp.arange(0, DR, 2, dtype=jnp.float32) / DR)
            t = jnp.arange(S, dtype=jnp.float32)
            fr = t[:, None] * inv[None, :]
            emb = jnp.concatenate([fr, fr], axis=-1)
            _CS_CACHE["cs"] = (np.asarray(jnp.cos(emb)), np.asarray(jnp.sin(emb)))
    return _CS_CACHE["cs"]


def _strip_major(wT, strips_w=128):
    Kd, M = wT.shape
    nkc, nst = Kd // 128, M // strips_w
    a = wT.reshape(nkc, 128, nst, strips_w)
    return np.ascontiguousarray(a.transpose(2, 1, 0, 3).reshape(nst, 128, nkc * strips_w))


def host_prep(ins, core):
    f4 = np.float32
    import ml_dtypes
    bf = ml_dtypes.bfloat16
    A = lambda v: np.asarray(v)
    d = {}
    x = A(ins["L0_attn_x"])[2 * core:2 * core + 2]
    d["x_in"] = np.ascontiguousarray(x.transpose(0, 2, 1))
    d["w_inp"] = _strip_major(np.ascontiguousarray(A(ins["L0_attn_in_proj_w"]).T))
    xp = np.ascontiguousarray(A(ins["L0_attn_x_proj_w"]).T)
    xp2 = np.zeros((D, 1152), f4)
    xp2[:, :1024] = xp[:, 32:1056]
    xp2[:, 1024:1040] = xp[:, 0:16]
    xp2[:, 1040:1056] = xp[:, 16:32]
    d["w_xp"] = _strip_major(xp2)
    d["w_outp"] = _strip_major(np.ascontiguousarray(A(ins["L0_attn_out_proj_w"]).T))
    d["conv_w"] = np.ascontiguousarray(A(ins["L0_attn_conv_w"]).reshape(D, K_CONV)
                                       .reshape(Nd, 128, K_CONV).transpose(1, 0, 2)
                                       .reshape(128, Nd * K_CONV))
    d["conv_b"] = np.ascontiguousarray(A(ins["L0_attn_conv_b"]).reshape(Nd, 128).T)
    d["d_par"] = np.ascontiguousarray(A(ins["L0_attn_D_param"]).reshape(Nd, 128).T)
    d["w_g0"] = np.ascontiguousarray(A(ins["L0_ffn_w_gate"]).T.reshape(Nd, 128, I0))
    d["w_u0"] = np.ascontiguousarray(A(ins["L0_ffn_w_up"]).T.reshape(Nd, 128, I0))
    d["w_d0"] = np.ascontiguousarray(A(ins["L0_ffn_w_down"]).T.reshape(I0 // 128, 128, D))
    d["w_q"] = _strip_major(np.ascontiguousarray(A(ins["L1_attn_wq"])))
    d["w_qc"] = _strip_major(np.ascontiguousarray(A(ins["L1_attn_wqc"])))
    d["w_qr"] = _strip_major(np.ascontiguousarray(A(ins["L1_attn_wqr"])))
    d["w_qr_p"] = _strip_major(np.ascontiguousarray(np.roll(A(ins["L1_attn_wqr"]), -64, axis=1)))
    d["w_kv"] = _strip_major(np.ascontiguousarray(A(ins["L1_attn_wkv"])))
    d["w_kr"] = _strip_major(np.ascontiguousarray(A(ins["L1_attn_wkr"])))
    d["w_kr_p"] = _strip_major(np.ascontiguousarray(np.roll(A(ins["L1_attn_wkr"]), -64, axis=1)))
    d["w_kc"] = _strip_major(np.ascontiguousarray(A(ins["L1_attn_wkc"])))
    d["w_v"] = np.ascontiguousarray(A(ins["L1_attn_wv"]).reshape(2, 128, D))
    d["w_o"] = _strip_major(np.ascontiguousarray(A(ins["L1_attn_wo"])))
    for nm, key in (("b_q", "L1_attn_bq"), ("b_qc", "L1_attn_bqc"),
                    ("b_kv", "L1_attn_bkv"), ("b_kc", "L1_attn_bkc"),
                    ("b_v", "L1_attn_bv"), ("b_o", "L1_attn_bo")):
        d[nm] = np.ascontiguousarray(A(ins[key]).reshape(-1, 128))
    d["b_qr"] = np.ascontiguousarray(A(ins["L1_attn_bqr"]).reshape(1, 128))
    d["b_qr_p"] = np.ascontiguousarray(np.roll(A(ins["L1_attn_bqr"]), -64).reshape(1, 128))
    d["b_kr"] = np.ascontiguousarray(A(ins["L1_attn_bkr"]).reshape(1, 128))
    d["b_kr_p"] = np.ascontiguousarray(np.roll(A(ins["L1_attn_bkr"]), -64).reshape(1, 128))
    d["w_g1"] = np.ascontiguousarray(A(ins["L1_ffn_w_gate"]).T.reshape(Nd, 128, I1))
    d["w_u1"] = np.ascontiguousarray(A(ins["L1_ffn_w_up"]).T.reshape(Nd, 128, I1))
    d["w_d1"] = np.ascontiguousarray(A(ins["L1_ffn_w_down"]).T.reshape(I1 // 128, 128, D))
    cos_t, sin_t = _rope_tables()
    cosT = np.ascontiguousarray(cos_t.T)
    sinT = sin_t.T.copy()
    sinT[:64] *= -1.0
    d["cosT"] = cosT
    d["sinT"] = np.ascontiguousarray(sinT)
    d["tri"] = (np.arange(128)[:, None] <= np.arange(128)[None, :]).astype(f4).astype(bf)
    bo = np.zeros((Nd, 128, H), f4)
    for kc in range(Nd):
        bo[kc, 0:64, 2 * kc] = 1.0
        bo[kc, 64:128, 2 * kc + 1] = 1.0
    d["blockones"] = np.ascontiguousarray(bo.transpose(1, 0, 2).reshape(128, Nd * H)).astype(bf)
    e64 = np.zeros((65, 64), f4)
    e64[64, :] = 1.0
    d["e64"] = e64.astype(bf)
    d["pidx"] = np.tile(np.arange(32, dtype=f4)[:, None], (1, 128)).astype(bf)
    indbc = np.zeros((32, 4096), f4)
    for n in range(16):
        indbc[n, 128 * n:128 * (n + 1)] = 1.0
        indbc[16 + n, 2048 + 128 * n:2048 + 128 * (n + 1)] = 1.0
    d["indbc"] = indbc.astype(bf)
    e8 = np.zeros((8, 8, 128), f4)
    for j in range(8):
        e8[j, j, :] = 1.0
    d["e8"] = np.ascontiguousarray(e8.transpose(1, 0, 2).reshape(8, 8 * 128))
    d["ident"] = np.eye(128, dtype=f4).astype(bf)
    d["ident_f"] = np.eye(128, dtype=f4)
    return d


class Prog:
    def __init__(self):
        nc = bass.Bass("TRN2", target_bir_lowering=False, debug=False,
                       dynamic_dma_scratch_size=8192)
        self.nc = nc
        self.k = KB(nc)
        self.D = {}
        self._mkio()
        self._mksbuf()
        self.t_carry = {}
        self.t_tail = [None] * Nd
        self.t_stripbuf = {}
        self.t_sc_guard = {}
        self.t_out_last = None
        self.t_fw = None
        self.bar = []

    def _mkio(self):
        nc = self.nc

        def din(name, shape, dt=F32):
            self.D[name] = nc.dram_tensor("in_" + name, list(shape), dt, kind="ExternalInput").ap()
        din("x_in", (B_CORE, D, S))
        din("w_inp", (16, 128, 1024)); din("w_xp", (9, 128, 1024)); din("w_outp", (8, 128, 1024))
        din("conv_w", (128, Nd * K_CONV)); din("conv_b", (128, Nd)); din("d_par", (128, Nd))
        din("w_g0", (Nd, 128, I0)); din("w_u0", (Nd, 128, I0)); din("w_d0", (I0 // 128, 128, D))
        din("w_q", (2, 128, 1024)); din("w_qc", (7, 128, 256)); din("w_qr", (1, 128, 1024))
        din("w_qr_p", (1, 128, 1024)); din("w_kv", (2, 128, 1024)); din("w_kr", (1, 128, 1024))
        din("w_kr_p", (1, 128, 1024)); din("w_kc", (7, 128, 256)); din("w_v", (2, 128, D))
        din("w_o", (8, 128, 1024))
        din("b_q", (2, 128)); din("b_qc", (7, 128)); din("b_qr", (1, 128)); din("b_qr_p", (1, 128))
        din("b_kv", (2, 128)); din("b_kr", (1, 128)); din("b_kr_p", (1, 128)); din("b_kc", (7, 128))
        din("b_v", (8, 128)); din("b_o", (8, 128))
        din("w_g1", (Nd, 128, I1), I32); din("w_u1", (Nd, 128, I1), I32)
        din("w_d1", (I1 // 128, 128, D), I32)
        din("cosT", (128, S), F32); din("sinT", (128, S), F32); din("tri", (128, 128), BF16)
        din("blockones", (128, Nd * H), BF16); din("e64", (65, 64), BF16)
        din("e8", (8, 8 * 128), F32); din("ident", (128, 128), BF16)
        din("pidx", (32, 128), BF16); din("indbc", (32, 4096), BF16)
        din("ident_f", (128, 128), F32)
        self.out_h = nc.dram_tensor("out", [B_CORE, D, S], F32, kind="ExternalOutput").ap()
        scr = {}
        scr["g0"] = nc.dram_tensor("scr_g0", [Nd, 128, I0], BF16, kind="Internal").ap()
        scr["u0"] = nc.dram_tensor("scr_u0", [Nd, 128, I0], BF16, kind="Internal").ap()
        scr["d0"] = nc.dram_tensor("scr_d0", [I0 // 128, 128, D], BF16, kind="Internal").ap()
        scr["g1"] = nc.dram_tensor("scr_g1", [Nd, 128, I1], BF16, kind="Internal").ap()
        scr["u1"] = nc.dram_tensor("scr_u1", [Nd, 128, I1], BF16, kind="Internal").ap()
        scr["d1"] = nc.dram_tensor("scr_d1", [I1 // 128, 128, D], BF16, kind="Internal").ap()
        for nm, shp in (("inp", (16, 128, 1024)), ("xp", (9, 128, 1024)),
                        ("outp", (8, 128, 1024)), ("q", (2, 128, 1024)),
                        ("qc", (7, 128, 256)), ("qr", (1, 128, 1024)),
                        ("qr_p", (1, 128, 1024)), ("kv", (2, 128, 1024)),
                        ("kc", (7, 128, 256)), ("kr", (1, 128, 1024)),
                        ("kr_p", (1, 128, 1024)), ("v", (2, 128, 1024)),
                        ("o", (8, 128, 1024))):
            scr[nm] = nc.dram_tensor("scr_" + nm, list(shp), BF16, kind="Internal").ap()
        scr["kk"] = nc.dram_tensor("scr_kk", [8, 128, 2048], BF16, kind="Internal").ap()
        scr["vv"] = nc.dram_tensor("scr_vv", [16, 128, 1040], BF16, kind="Internal").ap()
        self.scr = scr

    def _mksbuf(self):
        nc = self.nc
        self.X = [nc.alloc_sbuf_tensor(f"X{i}", [128, TOK], BF16).ap() for i in range(Nd)]
        self.BIG = nc.alloc_sbuf_tensor("BIGP", [128, NBIG], BF16).ap()
        self.FP = nc.alloc_sbuf_tensor("FPP", [128, NFP], F32).ap()
        self.IP = nc.alloc_sbuf_tensor("IPP", [128, 1024], I32).ap()
        for nm, shape, dt in (("tri", [128, 128], BF16), ("blockones", [128, Nd * H], BF16),
                              ("e64", [65, 64], BF16), ("e8", [8, 8 * 128], F32),
                              ("ident", [128, 128], BF16), ("ident_f", [128, 128], F32),
                              ("conv_w", [128, Nd * K_CONV], F32), ("conv_b", [128, Nd], F32),
                              ("d_par", [128, Nd], F32), ("bias_s", [128, 40], F32),
                              ("ones_cb", [128, 1], BF16), ("ones_cf", [128, 1], F32),
                              ("ones_rf", [1, 128], F32),
                              ("ones_rb", [1, 128], BF16),
                              ("ind32", [32, 128], BF16), ("pidx", [32, 128], BF16),
                              ("indbc", [32, 4096], BF16),
                              ("probe", [128, 1], F32),
                              ("probe_i", [128, 1], I32), ("shift", [128, 1], F32),
                              ("hcar", [128, Nd * N_STATE], BF16), ("trcp", [65, 512], BF16),
                              ("rowt", [1, 16], F32), ("negm", [128, 16], F32),
                              ("knmax", [16, 2], F32), ("qnmax", [16, 1], F32),
                              ("mrow", [16, 1], F32), ("fconst", [128, 2], F32)):
            setattr(self, nm, nc.alloc_sbuf_tensor(nm, shape, dt).ap())
        self.psum = [nc.alloc_psum_tensor(f"ps{i}", [128, 512], F32).ap() for i in range(8)]
        o = 0

        def al(n):
            nonlocal o
            r = o
            o += n
            assert o <= NBIG, o
            return r
        self.A_XCW = [al(1027) for _ in range(Nd)]
        self.A_XC = [al(CH) for _ in range(Nd)]
        self.A_Z = [al(CH) for _ in range(Nd)]
        self.A_W = [al(CH) for _ in range(Nd)]
        self.A_SC = [al(CH) for _ in range(4)]
        self.A_BC = [al(CH) for _ in range(2)]
        self.A_WS = [al(1024) for _ in range(2)]
        self.A_WS2 = [al(1024) for _ in range(2)]
        self.A_SG = al(512)
        self.A_Q = al(512)
        self.A_MISC = al(64)
        self.A_BCS = al(1024)
        self.A_TL = al(64)
        self.big_used = o
        self.R0 = self.A_XCW[0]
        fo = 0

        def fl(n):
            nonlocal fo
            r = fo
            fo += n
            assert fo <= NFP, fo
            return r
        self.F_ROW = fl(192)
        self.F_SC = [fl(512) for _ in range(3)]
        self.F_Y = [fl(1024) for _ in range(2)]
        self.F_W = fl(1024)
        self.F_W2 = fl(512)

    def mm(self, deps, out, lhsT, rhs, start, stop, inc=None):
        return self.k.op("pe", lambda e: e.matmul(out, lhsT, rhs, start=start, stop=stop),
                         deps=deps, inc=inc)

    def dma(self, deps, out, in_, sem="dfw", eng="sp", allow_nc=False):
        if allow_nc:
            return self.k.op(eng, lambda e: e.dma_start(out=out, in_=in_,
                                                        allow_slow_non_contiguous=True),
                             deps=deps, dinc=sem)
        return self.k.op(eng, lambda e: e.dma_start(out=out, in_=in_), deps=deps, dinc=sem)

    def B(self, c0, w):
        return self.BIG[:, c0:c0 + w]

    @staticmethod
    def badd(e, out, in0, bias_ap):
        a, b = broadcast_tensor_aps(in0, bias_ap)
        return e.tensor_add(out, a, b)

    def F(self, c0, w):
        return self.FP[:, c0:c0 + w]

    def stream_strip_f32(self, deps, src_ap, width, slot):
        k = self.k
        base = (self.A_WS[slot] if slot < 2 else self.A_WS2[slot - 2])
        dst = self.B(base, width)
        prev = self.t_stripbuf.get(slot)
        toks = []
        for piece in range(0, width, 1024):
            w = min(1024, width - piece)
            tl = self.dma(list(deps) + [prev, self.t_fw] + toks[-1:], self.F(self.F_W, w),
                          src_ap[:, piece:piece + w])
            tc = k.op("act", lambda e, piece=piece, w=w, dst=dst: e.copy(
                dst[:, piece:piece + w], self.F(self.F_W, w)), deps=[tl], inc="act")
            self.t_fw = tc
            toks.append(tc)
        return dst, toks[-1]

    def stream_strip_bf16(self, deps, scr_ap, kc0, nkc, m, slot):
        base = (self.A_WS[slot] if slot < 2 else self.A_WS2[slot - 2])
        dst = self.B(base, nkc * 128)
        prev = self.t_stripbuf.get(slot)
        srcv = scr_ap[kc0:kc0 + nkc, :, 128 * m:128 * (m + 1)].rearrange("a p j -> p a j")
        dstv = dst.rearrange("p (a j) -> p a j", j=128)
        tl = self.dma(list(deps) + [prev], dstv, srcv, sem="d%d" % slot, eng="sp")
        return dst, tl

    def stream_w(self, deps, name, strip, width, slot):
        base = (self.A_WS[slot] if slot < 2 else self.A_WS2[slot - 2])
        dst = self.B(base, width)
        prev = self.t_stripbuf.get(slot)
        tl = self.dma(list(deps) + [prev, self.t_wconv[name]], dst,
                      self.scr[name][strip][:, 0:width], sem="d%d" % slot, eng="sp")
        return dst, tl

    def strip_done(self, slot, tok):
        self.t_stripbuf[slot] = tok

    def emit_init(self):
        k, D_ = self.k, self.D
        tc = []
        for nm in ("tri", "blockones", "e64", "e8", "ident", "ident_f", "pidx", "indbc",
                   "conv_w", "conv_b", "d_par"):
            tc.append(self.dma([], getattr(self, nm)[:, :], D_[nm][:, :]))
        self.bias_cols = {"b_q": 0, "b_qc": 2, "b_qr": 9, "b_qr_p": 10, "b_kv": 11, "b_kr": 13,
                          "b_kr_p": 14, "b_kc": 15, "b_v": 22, "b_o": 30}
        for nm, c0 in self.bias_cols.items():
            n = self.D[nm].shape[0]
            tc.append(self.dma([], self.bias_s[:, c0:c0 + n], D_[nm].rearrange("a b -> b a"),
                                  allow_nc=True))
        self.t_const = tc[-1]
        p1 = k.op("dve", lambda e: e.memset(self.probe[:, :], 0.7), inc="dve")
        p2 = k.op("dve", lambda e: e.tensor_copy(self.probe_i[:, :], self.probe[:, :]),
                  deps=[p1], inc="dve")
        p3 = k.op("dve", lambda e: e.tensor_copy(self.probe[:, :], self.probe_i[:, :]),
                  deps=[p2], inc="dve")
        self.t_shift = k.op("dve", lambda e: e.tensor_scalar(self.shift[:, :], self.probe[:, :],
                                                             -0.5, 0.5, OP.mult, OP.add),
                            deps=[p3], inc="dve")
        k.op("dve", lambda e: e.memset(self.ones_cb[:, :], 1.0), inc="dve")
        k.op("dve", lambda e: e.memset(self.ones_cf[:, :], 1.0), inc="dve")
        k.op("dve", lambda e: e.memset(self.ones_rb[:, :], 1.0), inc="dve")
        self.t_ones = k.op("dve", lambda e: e.memset(self.ones_rf[:, :], 1.0), inc="dve")
        self.t_trcp0 = k.op("dve", lambda e: e.memset(self.trcp[0:64, :], 0.0), inc="dve")
        k.op("dve", lambda e: e.memset(self.fconst[:, 0:1], float(LN_EPS)), inc="dve")
        self.t_fc = k.op("dve", lambda e: e.memset(self.fconst[:, 1:2], 1.0), inc="dve")
        self.t_x = [[None] * Nd for _ in range(B_CORE)]
        stg = [self.F(self.F_W, 1024), self.F(self.F_Y[0], 1024)]
        tstg = [self.t_const, self.t_const]
        idx = 0
        for b in range(B_CORE):
            for i in range(Nd):
                th = None
                for half in range(2):
                    ci = idx % 2
                    idx += 1
                    tl = self.dma([tstg[ci]], stg[ci],
                                  D_["x_in"][b, 128 * i:128 * (i + 1), 1024 * half:1024 * (half + 1)],
                                  sem="d%d" % (2 + ci), eng="act")
                    th = k.op("act", lambda e, b=b, i=i, half=half, ci=ci: e.copy(
                        self.X[i][:, b * S + 1024 * half:b * S + 1024 * (half + 1)], stg[ci]),
                        deps=[tl, th], inc="act")
                    tstg[ci] = th
                self.t_x[b][i] = th
        self.t_stg = tstg
        self.t_xload = th
        self.bar = k.barrier_tokens()

    def emit_weight_prep(self):
        k, D_ = self.k, self.D
        stg = [self.F(self.F_W, 1024), self.F(self.F_Y[0], 1024)]
        qb = [self.B(self.A_WS[0], 1024), self.B(self.A_WS[1], 1024)]
        gb = [self.B(self.A_WS2[0], 1024), self.B(self.A_WS2[1], 1024)]
        tstg = list(self.t_stg)
        tq = [None, None]
        tgb = [None, None]
        tst = [None, None]
        self.t_wconv = {}
        self.t_prep_ps = None
        idx = 0

        def pieces(n_k, n_m, pw=1024):
            out = []
            for kc in range(n_k):
                for p0 in range(0, n_m, pw):
                    out.append((kc, p0, min(pw, n_m - p0)))
            return out

        def conv_f32(nm, src):
            nonlocal idx
            dst = self.scr[nm]
            n_k, _, n_m = dst.shape
            for kc, p0, w in pieces(n_k, n_m):
                ci = idx % 2
                idx += 1
                tl = self.dma([tstg[ci]], stg[ci][:, 0:w], src[kc, :, p0:p0 + w],
                              sem="d%d" % ci, eng="act")
                tcv = k.op("act", lambda e, w=w, ci=ci: e.copy(qb[ci][:, 0:w], stg[ci][:, 0:w]),
                           deps=[tl, tst[ci], tq[ci]], inc="act")
                tstg[ci] = tcv
                tq[ci] = tcv
                tst[ci] = self.dma([tcv], dst[kc, :, p0:p0 + w], qb[ci][:, 0:w],
                                   sem=("dw" if ci == 0 else "dm"))
            self.t_wconv[nm] = tst[(idx - 1) % 2]

        def quant_l0(src, dst, n_k, n_m, col):
            nonlocal idx
            acc = self.F(self.F_ROW + 16, 1)
            nacc = [self.F(self.F_ROW + 17, 1), self.F(self.F_ROW + 12, 1)]
            t_acc = None
            tr = [None, None]
            for kc, p0, w in pieces(n_k, n_m):
                ci = idx % 2
                idx += 1
                tl = self.dma([tstg[ci]], stg[ci][:, 0:w], src[kc, :, p0:p0 + w],
                              sem="d%d" % ci, eng="act")
                tr[ci] = k.op("dve", lambda e, ci=ci, w=w: e.reduce_sum(
                    nacc[ci], stg[ci][:, 0:w], AX, apply_absolute_value=True),
                    deps=[tl, t_acc], inc="dve")
                if t_acc is None:
                    t_acc = k.op("dve", lambda e, ci=ci: e.tensor_copy(acc, nacc[ci]),
                                 deps=[tr[ci]], inc="dve")
                else:
                    t_acc = k.op("dve", lambda e, ci=ci: e.tensor_add(acc, acc, nacc[ci]),
                                 deps=[tr[ci]], inc="dve")
                tstg[ci] = tr[ci]
            tm = self.mm([t_acc, self.t_ones, self.t_prep_ps], self.psum[7][0:1, 0:1], acc,
                         self.ones_cf[:, :], True, True, inc="pe")
            tmean = k.op("dve", lambda e: e.tensor_scalar(self.F(self.F_ROW + 18, 1)[0:1, :],
                                                          self.psum[7][0:1, 0:1],
                                                          1.0 / (n_k * 128 * n_m), EPS_Q,
                                                          OP.mult, OP.max),
                         deps=[tm], inc="dve")
            tbb = self.mm([tmean], self.psum[7][:, 1:2], self.ones_rf[:, :],
                          self.F(self.F_ROW + 18, 1)[0:1, :], True, True, inc="pe")
            t_sw = k.op("dve", lambda e, col=col: e.tensor_copy(self.F(self.F_ROW + 20 + col, 1),
                                                                self.psum[7][:, 1:2]),
                        deps=[tbb], inc="dve")
            t_swr = k.op("dve", lambda e, col=col: e.reciprocal(self.F(self.F_ROW + 26 + col, 1),
                                                                self.F(self.F_ROW + 20 + col, 1)),
                         deps=[t_sw], inc="dve")
            self.t_prep_ps = t_swr
            for kc, p0, w in pieces(n_k, n_m):
                ci = idx % 2
                idx += 1
                tl = self.dma([tstg[ci]], stg[ci][:, 0:w], src[kc, :, p0:p0 + w],
                              sem="d%d" % ci, eng="act")
                tt = k.op("dve", lambda e, col=col, ci=ci, w=w: e.tensor_scalar(
                    stg[ci][:, 0:w], stg[ci][:, 0:w], self.F(self.F_ROW + 26 + col, 1),
                    None, OP.mult),
                    deps=[tl, t_swr], inc="dve")
                tg = k.op("act", lambda e, ci=ci, w=w: e.activation(gb[ci][:, 0:w],
                                                                    stg[ci][:, 0:w], ACT.Sign),
                          deps=[tt, tgb[ci]], inc="act")
                ta = k.op("dve", lambda e, ci=ci, w=w: e.tensor_mul(stg[ci][:, 0:w],
                                                                    stg[ci][:, 0:w],
                                                                    gb[ci][:, 0:w]),
                          deps=[tg], inc="dve")
                tge = k.op("dve", lambda e, ci=ci, w=w: e.tensor_scalar(stg[ci][:, 0:w],
                                                                        stg[ci][:, 0:w],
                                                                        0.5, None, OP.is_ge),
                           deps=[ta], inc="dve")
                tqm = k.op("dve", lambda e, ci=ci, w=w: e.tensor_mul(qb[ci][:, 0:w],
                                                                     stg[ci][:, 0:w],
                                                                     gb[ci][:, 0:w]),
                           deps=[tge, tst[ci], tq[ci]], inc="dve")
                tstg[ci] = tqm
                tgb[ci] = tqm
                tq[ci] = tqm
                tst[ci] = self.dma([tqm], dst[kc, :, p0:p0 + w], qb[ci][:, 0:w],
                                   sem=("dw" if ci == 0 else "dm"))
            return tst[(idx - 1) % 2]

        ib = [self.IP[:, 0:512], self.IP[:, 512:1024]]
        tib = [None, None]

        def conv_l1(src, dst, n_k, n_m):
            nonlocal idx
            for kc, p0, w in pieces(n_k, n_m, 512):
                ci = idx % 2
                idx += 1
                tl = self.dma([tib[ci]], ib[ci], src[kc, :, p0:p0 + 512],
                              sem="d%d" % (2 + ci), eng="act")
                tcv = k.op("dve", lambda e, ci=ci: e.tensor_copy(qb[ci][:, 0:512], ib[ci]),
                           deps=[tl, tst[ci], tq[ci]], inc="dve")
                tib[ci] = tcv
                tq[ci] = tcv
                tst[ci] = self.dma([tcv], dst[kc, :, p0:p0 + 512], qb[ci][:, 0:512],
                                   sem=("dw" if ci == 0 else "dm"))
            return tst[(idx - 1) % 2]

        conv_f32("inp", D_["w_inp"])
        conv_f32("xp", D_["w_xp"])
        conv_f32("outp", D_["w_outp"])
        conv_f32("q", D_["w_q"])
        conv_f32("qc", D_["w_qc"])
        conv_f32("qr", D_["w_qr"])
        conv_f32("qr_p", D_["w_qr_p"])
        conv_f32("kv", D_["w_kv"])
        conv_f32("kc", D_["w_kc"])
        conv_f32("kr", D_["w_kr"])
        conv_f32("kr_p", D_["w_kr_p"])
        conv_f32("v", D_["w_v"])
        conv_f32("o", D_["w_o"])
        t1 = quant_l0(D_["w_g0"], self.scr["g0"], Nd, I0, 0)
        t2 = quant_l0(D_["w_u0"], self.scr["u0"], Nd, I0, 1)
        t3 = quant_l0(D_["w_d0"], self.scr["d0"], I0 // 128, D, 2)
        t4 = conv_l1(D_["w_g1"], self.scr["g1"], Nd, I1)
        t5 = conv_l1(D_["w_u1"], self.scr["u1"], Nd, I1)
        t6 = conv_l1(D_["w_d1"], self.scr["d1"], I1 // 128, D)
        self.t_scr = {"g0": t1, "u0": t2, "d0": t3, "g1": t4, "u1": t5, "d1": t6}
        self.t_wconv.update(self.t_scr)
        self.t_stripbuf[0] = tst[0]
        self.t_stripbuf[1] = tst[1]
        self.t_stripbuf[2] = tgb[0]
        self.t_stripbuf[3] = tgb[1]
        self.t_fw = k.op("dve", lambda e: e.memset(self.F(self.F_ROW + 13, 1), 0.0),
                         deps=[tstg[0], tstg[1], tq[0], tq[1], tgb[0], tgb[1],
                               tst[0], tst[1], self.t_prep_ps], inc="dve")

    def emit_layernorm(self, deps, xs_tiles, x_out_tiles, f32_dma=None):
        k = self.k
        r0 = self.F(self.F_SC[0], 512)[0:1, :]
        r1 = self.F(self.F_SC[1], 512)[0:1, :]
        r2 = self.F(self.F_SC[2], 512)[0:1, :]
        tsum = None
        for i in range(Nd):
            dd = (list(deps) + list(self.t_ln_ps) + [self.t_fw]) if i == 0 else [tsum]
            tsum = self.mm(dd, self.psum[6][0:1, :], self.ones_cb[:, :], xs_tiles[i],
                           start=(i == 0), stop=(i == Nd - 1), inc="pe")
        tss = None
        tg = None
        for i in range(Nd):
            tg = k.op("act", lambda e, i=i: e.activation(self.B(self.A_SG, 512), xs_tiles[i],
                                                         ACT.Square),
                      deps=list(deps) + [tg, tss, self.t_fw if i == 0 else None], inc="act")
            tss = self.mm([tg], self.psum[7][0:1, :], self.ones_cb[:, :], self.B(self.A_SG, 512),
                          start=(i == 0), stop=(i == Nd - 1), inc="pe")
        t_m = k.op("dve", lambda e: e.tensor_scalar(r0, self.psum[6][0:1, :], 1.0 / D, None, OP.mult),
                   deps=[tsum, self.t_ln_app], inc="dve")
        t_q = k.op("dve", lambda e: e.tensor_scalar(r1, self.psum[7][0:1, :], 1.0 / D, None, OP.mult),
                   deps=[tss, self.t_ln_app], inc="dve")
        t_m2 = k.op("dve", lambda e: e.tensor_mul(r2, r0, r0), deps=[t_m], inc="dve")
        t_v = k.op("dve", lambda e: e.tensor_sub(r1, r1, r2), deps=[t_q, t_m2], inc="dve")
        t_sd = k.op("act", lambda e: e.activation(r1, r1, ACT.Sqrt, bias=self.fconst[0:1, 0:1]),
                    deps=[t_v, self.t_fc], inc="act")
        t_a = k.op("dve", lambda e: e.reciprocal(r1, r1), deps=[t_sd], inc="dve")
        t_b = k.op("dve", lambda e: e.scalar_tensor_tensor(r2, r0, -1.0, r1, OP.mult, OP.mult),
                   deps=[t_m, t_a], inc="dve")
        tA = self.mm([t_a], self.psum[6][:, :], self.ones_rf[:, :], r1, True, True, inc="pe")
        tB = self.mm([t_b, tA], self.psum[7][:, :], self.ones_rf[:, :], r2, True, True, inc="pe")
        abc = self.F(self.F_SC[0], 512)
        bbc = self.F(self.F_SC[1], 512)
        te1 = k.op("dve", lambda e: e.tensor_copy(abc, self.psum[6][:, :]),
                   deps=[tA, t_b], inc="dve")
        te2 = k.op("dve", lambda e: e.tensor_copy(bbc, self.psum[7][:, :]), deps=[tB], inc="dve")
        self.t_ln_ps = (te1, te2)
        touts = []
        tp = None
        for i in range(Nd):
            tt = k.op("dve", lambda e, i=i: e.tensor_mul(self.F(self.F_Y[0], 512), xs_tiles[i], abc),
                      deps=[te1, tp], inc="dve")
            if f32_dma is None:
                t2 = k.op("dve", lambda e, i=i: e.tensor_add(x_out_tiles[i], self.F(self.F_Y[0], 512),
                                                             bbc),
                          deps=[tt, te2], inc="dve")
                touts.append(t2)
                tp = t2
            else:
                b, t0 = f32_dma
                fo = self.F(self.F_Y[1], 512)
                t2 = k.op("dve", lambda e: e.tensor_add(fo, self.F(self.F_Y[0], 512), bbc),
                          deps=[tt, te2, self.t_out_last], inc="dve")
                t3 = self.dma([t2], self.out_h[b, 128 * i:128 * (i + 1), t0:t0 + 512], fo, sem="dst")
                self.t_out_last = t3
                touts.append(t3)
                tp = t2
        self.t_ln_app = tp
        return touts

    def emit_col_scales(self, deps, tiles, n_tiles, save_c2=False, tile_deps=None):
        k = self.k
        g = self.t_sc_guard
        gd = [g.get(("B", 0)), g.get(("Br", 0)), g.get(("C", 0)), g.get(("Cr", 0))]
        mx = self.B(self.A_BC[0], 512)
        mn = self.B(self.A_BC[1], 512)
        td = tile_deps if tile_deps is not None else [None] * n_tiles
        t1 = k.op("dve", lambda e: e.tensor_max(mx, tiles[0], tiles[1]),
                  deps=list(deps) + [td[0], td[1]] + gd, inc="dve")
        t2 = k.op("dve", lambda e: e.tensor_tensor(mn, tiles[0], tiles[1], OP.min),
                  deps=list(deps) + [t1], inc="dve")
        for i in range(2, n_tiles):
            t1 = k.op("dve", lambda e, i=i: e.tensor_max(mx, mx, tiles[i]),
                      deps=[t1, td[i]], inc="dve")
            t2 = k.op("dve", lambda e, i=i: e.tensor_tensor(mn, mn, tiles[i], OP.min),
                      deps=[t2, td[i]], inc="dve")
        g[("B", 0)] = [g.get(("B", 0)), t1]
        g[("C", 0)] = [g.get(("C", 0)), t2]
        tam = k.op("dve", lambda e: e.scalar_tensor_tensor(mx, mn, -1.0, mx, OP.mult, OP.max),
                   deps=[t1, t2], inc="dve")
        amax_c = self.F(self.F_ROW, 4)
        tcols = []
        for j in range(4):
            tcv = k.op("dve", lambda e, j=j: e.tensor_copy(self.F(self.F_W, 128),
                                                           mx[:, 128 * j:128 * (j + 1)]),
                       deps=[tam, self.t_fw] + tcols[-1:], inc="dve")
            self.t_fw = tcv
            ttr = self.k.op("pe", lambda e: e.transpose(
                self.psum[4][:, 0:128], self.F(self.F_W, 128), self.ident_f[:, :]),
                deps=[tcv], inc="pe")
            tr = k.op("dve", lambda e, j=j: e.reduce_max(amax_c[:, j:j + 1],
                                                         self.psum[4][:, 0:128], AX),
                      deps=[ttr], inc="dve")
            self.t_fw = tr
            tcols.append(tr)
        tclip = k.op("dve", lambda e: e.tensor_scalar(amax_c, amax_c, EPS_Q, None, OP.max),
                     deps=tcols, inc="dve")
        s_col = self.F(self.F_ROW + 4, 4)
        c_col = self.F(self.F_ROW + 8, 4)
        ts1 = k.op("dve", lambda e: e.reciprocal(s_col, amax_c), deps=[tclip], inc="dve")
        ts2 = k.op("dve", lambda e: e.tensor_scalar(s_col, s_col, 127.0, None, OP.mult),
                   deps=[ts1], inc="dve")
        tc1 = k.op("dve", lambda e: e.tensor_scalar(c_col, amax_c, 1.0 / 127.0, None, OP.mult),
                   deps=[tclip], inc="dve")
        pk = self.F(self.F_ROW + 32, 8)
        tp1 = k.op("dve", lambda e: e.tensor_copy(pk[:, 0:4], s_col), deps=[ts2], inc="dve")
        tp2 = k.op("dve", lambda e: e.tensor_copy(pk[:, 4:8], c_col), deps=[tc1, tp1], inc="dve")
        ttr2 = self.k.op("pe", lambda e: e.transpose(self.psum[4][0:8, 0:128], pk,
                                                     self.ident_f[:, :]),
                         deps=[tp2], inc="pe")
        rows = self.F(self.F_ROW + 40, 128)[0:8, :]
        tev = k.op("dve", lambda e: e.tensor_copy(rows, self.psum[4][0:8, 0:128]),
                   deps=[ttr2], inc="dve")
        c_dst = self.F_SC[2] if save_c2 else self.F_SC[1]
        t_s = t_c = None
        for j in range(4):
            tmS = self.mm([tev, t_s], self.psum[5][:, 128 * j:128 * (j + 1)],
                          self.e8[:, 128 * j:128 * (j + 1)], rows, True, True, inc="pe")
            t_s = k.op("dve", lambda e, j=j: e.tensor_copy(self.F(self.F_SC[0] + 128 * j, 128),
                                                           self.psum[5][:, 128 * j:128 * (j + 1)]),
                       deps=[tmS], inc="dve")
        for j in range(4):
            tmC = self.mm([tev, t_s, t_c], self.psum[5][:, 128 * j:128 * (j + 1)],
                          self.e8[:, 128 * (4 + j):128 * (4 + j + 1)], rows, True, True, inc="pe")
            t_c = k.op("dve", lambda e, j=j, c_dst=c_dst: e.tensor_copy(
                self.F(c_dst + 128 * j, 128), self.psum[5][:, 128 * j:128 * (j + 1)]),
                deps=[tmC], inc="dve")
        return t_s, t_c

    def emit_quant(self, deps, in_tiles, out_tiles, n_tiles, t_s):
        k = self.k
        toks = []
        tprev = None
        for i in range(n_tiles):
            tt = k.op("dve", lambda e, i=i: e.tensor_mul(self.F(self.F_W, 512), in_tiles[i],
                                                         self.F(self.F_SC[0], 512)),
                      deps=list(deps) + [t_s, tprev, self.t_fw], inc="dve")
            self.t_fw = tt
            tg = k.op("act", lambda e: e.activation(self.B(self.A_SG, 512), self.F(self.F_W, 512),
                                                    ACT.Sign),
                      deps=[tt], inc="act")
            ti = k.op("dve", lambda e: e.scalar_tensor_tensor(self.IP[:, 0:512],
                                                              self.B(self.A_SG, 512),
                                                              self.shift[:, :],
                                                              self.F(self.F_W, 512),
                                                              OP.mult, OP.add),
                      deps=[tg, self.t_shift], inc="dve")
            tq = k.op("dve", lambda e, i=i: e.tensor_scalar(out_tiles[i], self.IP[:, 0:512],
                                                            -128.0, 127.0, OP.max, OP.min),
                      deps=[ti], inc="dve")
            toks.append(tq)
            tprev = tq
            self.t_fw = tq
        return toks

    def emit_l0_chunk(self, b, c):
        k, D_ = self.k, self.D
        win = b * S + c * CH
        bar = self.bar
        evac = {}
        for m in range(16):
            ws, tw = self.stream_w(bar + [self.t_const], "inp", m, 1024, m % 2)
            for n2 in range(2):
                pt = self.psum[(m * 2 + n2) % 4]
                lastmm = None
                for kc in range(Nd):
                    lastmm = self.mm([tw, evac.get((m - 2, n2))] + bar, pt[:, :],
                                     ws[:, kc * 128:(kc + 1) * 128],
                                     self.X[kc][:, win + 512 * n2:win + 512 * (n2 + 1)],
                                     start=(kc == 0), stop=(kc == Nd - 1),
                                     inc="pe" if kc == Nd - 1 else None)
                if m < 8:
                    te = k.op("dve", lambda e, m=m, n2=n2, pt=pt: e.tensor_copy(
                        self.B(self.A_XCW[m] + 3 + 512 * n2, 512), pt[:, :]),
                        deps=[lastmm] + bar, inc="dve")
                else:
                    te = k.op("act", lambda e, m=m, n2=n2, pt=pt: e.activation(
                        self.B(self.A_Z[m - 8] + 512 * n2, 512), pt[:, :], ACT.Silu),
                        deps=[lastmm] + bar, inc="act")
                evac[(m, n2)] = te
            self.strip_done(m % 2, lastmm)
        conv_toks = []
        t_acc_last = None
        for i in range(Nd):
            base = self.A_XCW[i]
            if c == 0:
                tp = k.op("dve", lambda e, base=base: e.memset(self.B(base, 3), 0.0),
                          deps=[evac[(i, 0)]] + bar, inc="dve")
            else:
                tp = k.op("dve", lambda e, base=base, i=i: e.tensor_copy(
                    self.B(base, 3), self.B(self.A_TL + 8 * i, 3)),
                    deps=[self.t_tail[i], evac[(i, 0)]], inc="dve")
            acc = self.F(self.F_Y[1], CH)
            t0 = k.op("dve", lambda e, base=base, i=i: e.tensor_scalar(
                acc, self.B(base, CH), self.conv_w[:, i * 4:i * 4 + 1], None, OP.mult),
                deps=[tp, evac[(i, 0)], evac[(i, 1)], self.t_const, t_acc_last], inc="dve")
            for j in range(1, K_CONV):
                t0 = k.op("dve", lambda e, base=base, i=i, j=j: e.scalar_tensor_tensor(
                    acc, self.B(base + j, CH), self.conv_w[:, i * 4 + j:i * 4 + j + 1], acc,
                    OP.mult, OP.add), deps=[t0], inc="dve")
            tt = k.op("dve", lambda e, base=base, i=i: e.tensor_copy(
                self.B(self.A_TL + 8 * i, 3), self.B(base + CH, 3)), deps=[t0], inc="dve")
            self.t_tail[i] = tt
            ts = k.op("act", lambda e, i=i: e.activation(self.B(self.A_XC[i], CH), acc,
                                                         ACT.Silu, bias=self.conv_b[:, i:i + 1]),
                      deps=[t0] + bar, inc="act")
            t_acc_last = ts
            conv_toks.append(ts)
        dt_toks = [None] * Nd
        t_bc = None
        for m in range(9):
            ws, tw = self.stream_w([self.t_const], "xp", m, 1024, m % 2)
            for n2 in range(2):
                pt = self.psum[(m * 2 + n2) % 4]
                lastmm = None
                for kc in range(Nd):
                    lastmm = self.mm([tw, conv_toks[kc],
                                      dt_toks[m - 2][1] if m >= 2 and dt_toks[m - 2] else None],
                                     pt[:, :], ws[:, kc * 128:(kc + 1) * 128],
                                     self.B(self.A_XC[kc] + 512 * n2, 512),
                                     start=(kc == 0), stop=(kc == Nd - 1),
                                     inc="pe" if kc == Nd - 1 else None)
                if m < 8:
                    f1 = self.F(self.F_Y[0], 512)
                    ta = k.op("act", lambda e, pt=pt, f1=f1: e.activation(f1, pt[:, :], ACT.Abs),
                              deps=[lastmm, self.t_sp_prev, self.t_fw] + bar, inc="act")
                    tb = k.op("act", lambda e, f1=f1: e.activation(f1, f1, ACT.Exp, scale=-1.0),
                              deps=[ta], inc="act")
                    tc2 = k.op("act", lambda e, f1=f1: e.activation(f1, f1, ACT.Ln, bias=self.fconst[:, 1:2]),
                               deps=[tb, self.t_fc], inc="act")
                    td = k.op("dve", lambda e, m=m, n2=n2, pt=pt, f1=f1: e.scalar_tensor_tensor(
                        self.B(self.A_XCW[m] + 512 * n2, 512), pt[:, :], 0.0, f1, OP.max, OP.add),
                        deps=[tc2, conv_toks[m], self.t_tail[m]] + bar, inc="dve")
                    self.t_sp_prev = td
                    if n2 == 0:
                        dt_half = td
                    else:
                        dt_toks[m] = (dt_half, td)
                else:
                    t_bc = k.op("dve", lambda e, n2=n2, pt=pt: e.tensor_copy(
                        self.BIG[0:32, self.A_BCS + 512 * n2:self.A_BCS + 512 * (n2 + 1)],
                        pt[0:32, :]), deps=[lastmm, t_bc, self.t_fw] + bar, inc="dve")
            self.strip_done(m % 2, lastmm)
        t_w = [None] * Nd
        for i in range(Nd):
            wt = self.B(self.A_W[i], CH)
            t_w[i] = k.op("dve", lambda e, i=i, wt=wt: e.tensor_mul(
                wt, self.B(self.A_XCW[i], CH), self.B(self.A_XC[i], CH)),
                deps=[dt_toks[i][0], dt_toks[i][1], conv_toks[i]] + bar, inc="dve")
        BC = self.BIG[0:32, self.A_BCS:self.A_BCS + CH]
        g = self.t_sc_guard
        for key, slot in ((("B", 1), 2), (("Br", 1), 2), (("C", 1), 3), (("Cr", 1), 3)):
            g[key] = [g.get(key), self.t_stripbuf.get(slot)]
        t_dafree = [None, None]
        t_y3 = [None] * Nd
        t_yfree = None
        t_epi = None
        pB = [self.A_BC[0], self.A_WS2[0]]
        pC = [self.A_BC[1], self.A_WS2[1]]
        psB = [self.psum[4], self.psum[0]]
        psC = [self.psum[5], self.psum[1]]
        for i in range(Nd):
            t_n = None
            for n in range(N_STATE):
                p = n % 2
                sA = self.B(self.A_SC[0 + p], CH)
                sB = self.B(self.A_SC[2], CH)
                sH = self.B(self.A_SC[3], CH)
                sT = sB
                tBn = None
                for n2 in range(2):
                    tmb = self.mm([t_bc, tBn, g.get(("B", p)), g.get(("Br", p)), self.t_const],
                                  psB[p][:, :], self.indbc[:, 128 * n:128 * (n + 1)],
                                  BC[:, 512 * n2:512 * (n2 + 1)], True, True, inc="pe")
                    tBn = k.op("act", lambda e, n2=n2, p=p: e.copy(
                        self.B(pB[p] + 512 * n2, 512), psB[p][:, :]),
                        deps=[tmb, tBn], inc="act")
                tCn = None
                for n2 in range(2):
                    tmc = self.mm([t_bc, tCn, g.get(("C", p)), g.get(("Cr", p))],
                                  psC[p][:, :], self.indbc[:, 2048 + 128 * n:2048 + 128 * (n + 1)],
                                  BC[:, 512 * n2:512 * (n2 + 1)], True, True, inc="pe")
                    tCn = k.op("act", lambda e, n2=n2, p=p: e.copy(
                        self.B(pC[p] + 512 * n2, 512), psC[p][:, :]),
                        deps=[tmc, tCn], inc="act")
                g[("B", p)] = tBn
                g[("C", p)] = tCn
                tda = k.op("act", lambda e, i=i, n=n, sA=sA: e.activation(
                    sA, self.B(self.A_XCW[i], CH), ACT.Exp, scale=-float(n + 1)),
                    deps=[dt_toks[i][0], dt_toks[i][1], t_dafree[p]], inc="act")
                tdb = k.op("dve", lambda e, sB=sB, i=i, p=p: e.tensor_mul(
                    sB, self.B(self.A_W[i], CH), self.B(pB[p], CH)),
                    deps=[t_w[i], tBn, t_n], inc="dve")
                g[("Br", p)] = tdb
                init = 0.0 if c == 0 else self.hcar[:, i * N_STATE + n:i * N_STATE + n + 1]
                dd = [tda, tdb]
                if c == 1:
                    dd.append(self.t_carry.get((b, i, n)))
                tsc = k.op("dve", lambda e, sA=sA, sB=sB, sH=sH, init=init: e.tensor_tensor_scan(
                    sH, sA, sB, init, OP.mult, OP.add), deps=dd, inc="dve")
                t_dafree[p] = tsc
                if c == 0:
                    tcar = k.op("dve", lambda e, sH=sH, i=i, n=n: e.tensor_copy(
                        self.hcar[:, i * N_STATE + n:i * N_STATE + n + 1], sH[:, CH - 1:CH]),
                        deps=[tsc], inc="dve")
                    self.t_carry[(b, i, n)] = tcar
                tmul = k.op("dve", lambda e, sH=sH, sT=sT, p=p: e.tensor_mul(
                    sT, sH, self.B(pC[p], CH)), deps=[tsc, tCn], inc="dve")
                g[("Cr", p)] = tmul
                yap = self.F(self.F_Y[0], CH)
                if n == 0:
                    t_n = k.op("dve", lambda e, sT=sT, yap=yap: e.tensor_copy(yap, sT),
                               deps=[tmul, t_yfree], inc="dve")
                elif n % 2 == 0:
                    t_n = k.op("pool", lambda e, sT=sT, yap=yap: e.tensor_add(yap, yap, sT),
                               deps=[tmul, t_n], inc="pool")
                else:
                    t_n = k.op("dve", lambda e, sT=sT, yap=yap: e.tensor_add(yap, yap, sT),
                               deps=[tmul, t_n], inc="dve")
            ty2 = k.op("dve", lambda e, i=i: e.scalar_tensor_tensor(
                self.F(self.F_Y[1], CH), self.B(self.A_XC[i], CH), self.d_par[:, i:i + 1],
                self.F(self.F_Y[0], CH), OP.mult, OP.add),
                deps=[t_n, conv_toks[i], t_epi], inc="dve")
            t_yfree = ty2
            ty3 = k.op("dve", lambda e, i=i: e.tensor_mul(
                self.B(self.A_XC[i], CH), self.F(self.F_Y[1], CH), self.B(self.A_Z[i], CH)),
                deps=[ty2, evac[(8 + i, 0)], evac[(8 + i, 1)], t_w[i]], inc="dve")
            t_epi = ty3
            t_y3[i] = ty3
        self.t_stripbuf[2] = [g[("B", 1)], g[("Br", 1)]]
        self.t_stripbuf[3] = [g[("C", 1)], g[("Cr", 1)]]
        t_xs = [[None, None] for _ in range(Nd)]
        for m in range(Nd):
            ws, tw = self.stream_w(t_y3[-1:], "outp", m, 1024, m % 2)
            for n2 in range(2):
                pt = self.psum[(m * 2 + n2) % 4]
                lastmm = None
                for kc in range(Nd):
                    lastmm = self.mm([tw, t_y3[kc],
                                      t_xs[m - 2][n2] if m >= 2 else None], pt[:, :],
                                     ws[:, kc * 128:(kc + 1) * 128],
                                     self.B(self.A_XC[kc] + 512 * n2, 512),
                                     start=(kc == 0), stop=(kc == Nd - 1),
                                     inc="pe" if kc == Nd - 1 else None)
                te = k.op("dve", lambda e, m=m, n2=n2, pt=pt, win=win: e.tensor_add(
                    self.B(self.A_Z[m] + 512 * n2, 512), pt[:, :],
                    self.X[m][:, win + 512 * n2:win + 512 * (n2 + 1)]),
                    deps=[lastmm, t_y3[m]], inc="dve")
                t_xs[m][n2] = te
            self.strip_done(m % 2, lastmm)
        for n2 in range(2):
            xs_tiles = [self.B(self.A_Z[i] + 512 * n2, 512) for i in range(Nd)]
            xo_tiles = [self.X[i][:, win + 512 * n2:win + 512 * (n2 + 1)] for i in range(Nd)]
            self.emit_layernorm([t_xs[i][n2] for i in range(Nd)], xs_tiles, xo_tiles)
        self.bar = k.barrier_tokens()

    def emit_ffn0(self, b, c2):
        k = self.k
        win = b * S + c2 * FCH
        bar = self.bar
        A_H = [self.R0 + 512 * i for i in range(32)]
        A_QX = [self.A_Z[0] + 512 * i for i in range(Nd)]
        xw = [self.X[i][:, win:win + FCH] for i in range(Nd)]
        t_s, t_c = self.emit_col_scales(bar, xw, Nd, save_c2=True)
        qx = [self.B(A_QX[i], FCH) for i in range(Nd)]
        tq = self.emit_quant(bar + [t_s], xw, qx, Nd, t_s)
        t_h = [None] * 32
        gbuf = [self.A_SG, self.A_SC[0]]
        ubuf = [self.A_Q, self.A_SC[2]]
        t_sgfree = [tq[-1], bar]
        t_qfree = [None, bar]
        for m in range(32):
            mp = m % 2
            wsg, twg = self.stream_strip_bf16(bar + [self.t_scr["d0"]], self.scr["g0"], 0, Nd, m, 0)
            wsu, twu = self.stream_strip_bf16(bar + [self.t_scr["d0"]], self.scr["u0"], 0, Nd, m, 1)
            ptg, ptu = self.psum[m % 2], self.psum[2 + m % 2]
            lg = lu = None
            for kc in range(Nd):
                lg = self.mm([twg, tq[kc], t_h[m - 2] if m >= 2 else None],
                             ptg[:, 0:FCH], wsg[:, kc * 128:(kc + 1) * 128],
                             qx[kc], start=(kc == 0), stop=(kc == Nd - 1),
                             inc="pe" if kc == Nd - 1 else None)
            for kc in range(Nd):
                lu = self.mm([twu, tq[kc]], ptu[:, 0:FCH], wsu[:, kc * 128:(kc + 1) * 128],
                             qx[kc], start=(kc == 0), stop=(kc == Nd - 1),
                             inc="pe" if kc == Nd - 1 else None)
            tr = k.op("act", lambda e, ptg=ptg, mp=mp: e.activation(self.B(gbuf[mp], FCH),
                                                                    ptg[:, 0:FCH], ACT.Relu),
                      deps=[lg, t_sgfree[mp]], inc="act")
            tu = k.op("act", lambda e, ptu=ptu, mp=mp: e.copy(self.B(ubuf[mp], FCH),
                                                              ptu[:, 0:FCH]),
                      deps=[lu, t_qfree[mp]], inc="act")
            tr2 = k.op("dve", lambda e, mp=mp: e.tensor_mul(self.B(gbuf[mp], FCH),
                                                            self.B(gbuf[mp], FCH),
                                                            self.B(gbuf[mp], FCH)),
                       deps=[tr], inc="dve")
            th = k.op("dve", lambda e, m=m, mp=mp: e.tensor_mul(self.B(A_H[m], FCH),
                                                                self.B(gbuf[mp], FCH),
                                                                self.B(ubuf[mp], FCH)),
                      deps=[tr2, tu] + bar, inc="dve")
            t_sgfree[mp] = th
            t_qfree[mp] = th
            t_h[m] = th
            self.strip_done(0, lg)
            self.strip_done(1, lu)
        ht = [self.B(A_H[m], FCH) for m in range(32)]
        if globals().get('DEV_TAP3') and b == 0 and c2 == 0:
            if not hasattr(self, "dbg_h"):
                self.dbg_h = self.nc.dram_tensor("dbg", [4, 128, 512], F32, kind="ExternalOutput").ap()
                self.dbgf = self.nc.alloc_sbuf_tensor("dbgf", [128, 512], F32).ap()
            self.t_dbg = getattr(self, "t_dbg", None)
            for di, col in enumerate((A_H[0], A_H[4], A_QX[0], A_QX[7])):
                tcv = k.op("dve", lambda e, col=col: e.tensor_copy(self.dbgf, self.B(col, 512)),
                           deps=[t_h[31], tq[-1], self.t_dbg], inc="dve")
                self.t_dbg = self.dma([tcv], self.dbg_h[di], self.dbgf, sem="dm")
            t_h = [self.t_dbg if t is t_h[0] else t for t in t_h]
            t_h[0] = self.t_dbg
        t_s2, t_c2 = self.emit_col_scales([], ht, 32, tile_deps=t_h)
        cx = self.F(self.F_SC[2], FCH)
        ch_ = self.F(self.F_SC[1], FCH)
        ct = self.F(self.F_Y[0], FCH)
        t_ct = k.op("dve", lambda e: e.tensor_mul(ct, cx, cx),
                    deps=[t_c, t_c2, list(self.t_dn_guard)], inc="dve")
        t_ct = k.op("dve", lambda e: e.tensor_mul(ct, ct, cx), deps=[t_ct], inc="dve")
        t_ct = k.op("dve", lambda e: e.tensor_mul(ct, ct, ch_), deps=[t_ct], inc="dve")
        for col, p in ((0, 2), (1, 1), (2, 1)):
            for _ in range(p):
                t_ct = k.op("dve", lambda e, col=col: e.tensor_scalar(
                    ct, ct, self.F(self.F_ROW + 20 + col, 1), None, OP.mult),
                    deps=[t_ct], inc="dve")
        dn_guard = [[self.t_dn_guard[m], list(self.t_ln_ps), bar] for m in range(Nd)]
        last_mm = [None] * Nd
        tq2 = []
        for kc in range(32):
            tq2 += self.emit_quant([t_h[kc], t_s2], ht[kc:kc + 1], ht[kc:kc + 1], 1, t_s2)
            wsd, twd = self.stream_w([tq2[kc]], "d0", kc, 1024, 2 + kc % 2)
            for m in range(Nd):
                last_mm[m] = self.mm(
                    [twd, tq2[kc], dn_guard[m] if kc == 0 else None,
                     (t_h[30 + m % 2] if kc == 0 and m < 4 else None)],
                    self.psum[m][:, 0:FCH],
                    wsd[:, m * 128:(m + 1) * 128], ht[kc],
                    start=(kc == 0), stop=(kc == 31),
                    inc="pe" if m == Nd - 1 else None)
            self.strip_done(2 + kc % 2, last_mm[Nd - 1])
        t_xs = [None] * Nd
        for m in range(Nd):
            tmu = k.op("dve", lambda e, m=m: e.tensor_mul(self.F(self.F_Y[1], FCH),
                                                          self.psum[m][:, 0:FCH], ct),
                       deps=[last_mm[Nd - 1], t_ct, t_xs[m - 1] if m else None], inc="dve")
            txs = k.op("dve", lambda e, m=m, win=win: e.tensor_add(
                self.B(A_QX[m], FCH), self.F(self.F_Y[1], FCH), self.X[m][:, win:win + FCH]),
                deps=[tmu, tq[m]], inc="dve")
            t_xs[m] = txs
        self.t_dn_guard = [t_xs[m] for m in range(Nd)]
        if globals().get('DEV_TAP2') and b == 0 and c2 == 0:
            if not hasattr(self, "dbg_h"):
                self.dbg_h = self.nc.dram_tensor("dbg", [4, 128, 512], F32, kind="ExternalOutput").ap()
                self.dbgf = self.nc.alloc_sbuf_tensor("dbgf", [128, 512], F32).ap()
            self.t_dbg = getattr(self, "t_dbg", None)
            taps = [(self.B(A_QX[0], FCH), t_xs[0]), (self.B(A_H[0], FCH), t_xs[0]),
                    (ct, t_xs[0]), (self.B(A_H[31], FCH), t_xs[0])]
            for di, (col, dtok) in enumerate(taps):
                tcv = k.op("dve", lambda e, col=col: e.tensor_copy(self.dbgf, col),
                           deps=[dtok, self.t_dbg], inc="dve")
                self.t_dbg = self.dma([tcv], self.dbg_h[di], self.dbgf, sem="dm")
        xs_tiles = [self.B(A_QX[i], FCH) for i in range(Nd)]
        xo_tiles = [self.X[i][:, win:win + FCH] for i in range(Nd)]
        self.emit_layernorm(t_xs, xs_tiles, xo_tiles)
        self.bar = k.barrier_tokens()

    def emit_l1_kv(self, b):
        k, D_ = self.k, self.D
        bar = self.bar
        A_CQ = [self.R0 + 2048 * i for i in range(2)]
        A_CKV = [self.R0 + 4096 + 2048 * i for i in range(2)]
        A_KS = [self.A_XC[0] + 2048 * i for i in range(2)]
        self.A_COS = self.A_XC[0] + 4096
        self.A_SIN = self.A_XC[0] + 6144
        A_VT = [self.A_Z[0] + 1040 * i for i in range(2)]
        A_RT = [self.A_Z[0] + 2080 + 512 * i for i in range(2)]
        A_BB = self.A_Z[0] + 3104
        self.A_CQv = A_CQ
        win = b * S
        BQ = self.bias_cols
        tc = None
        for half, nm in ((0, "cosT"), (1, "sinT")):
            for piece in range(S // 512):
                tl = self.dma(list(bar) + [self.t_fw],
                              self.F(self.F_W, 512), D_[nm][:, 512 * piece:512 * (piece + 1)])
                dstc = (self.A_COS if half == 0 else self.A_SIN) + 512 * piece
                tc = k.op("act", lambda e, dstc=dstc: e.copy(self.B(dstc, 512),
                                                             self.F(self.F_W, 512)),
                          deps=[tl], inc="act")
                self.t_fw = tc
        self.t_cossin = tc
        t_pj = {}
        t_pj_seq = []
        for dsti, (dst, wnm, bc0) in enumerate(((A_CQ, "w_q", BQ["b_q"]),
                                                (A_CKV, "w_kv", BQ["b_kv"]))):
            for m in range(2):
                ws, tw = self.stream_w(bar, {"w_q": "q", "w_kv": "kv"}[wnm], m, 1024, m % 2)
                for n4 in range(4):
                    pt = self.psum[n4 % 2]
                    lastmm = None
                    for kc in range(Nd):
                        lastmm = self.mm([tw, t_pj_seq[-2] if len(t_pj_seq) >= 2 else None]
                                         + bar, pt[:, :],
                                         ws[:, kc * 128:(kc + 1) * 128],
                                         self.X[kc][:, win + 512 * n4:win + 512 * (n4 + 1)],
                                         start=(kc == 0), stop=(kc == Nd - 1),
                                         inc="pe" if kc == Nd - 1 else None)
                    t_pj[(dsti, m, n4)] = k.op("act", lambda e, dst=dst, m=m, n4=n4, pt=pt, bc0=bc0:
                                               e.activation(self.B(dst[m] + 512 * n4, 512), pt[:, :],
                                                            ACT.Gelu,
                                                            bias=self.bias_s[:, bc0 + m:bc0 + m + 1]),
                                               deps=[lastmm] + bar, inc="act")
                    t_pj_seq.append(t_pj[(dsti, m, n4)])
                self.strip_done(m % 2, lastmm)
        t_ckv = [t_pj[(1, m, n4)] for m in range(2) for n4 in range(4)]
        self.t_cq = [t_pj[(0, m, n4)] for m in range(2) for n4 in range(4)]
        t_bv = None
        for s4 in range(8):
            tl = self.dma([self.t_bvrow_free if s4 else self.t_fw], self.F(self.F_W, 128)[0:1, :],
                          D_["b_v"][s4:s4 + 1, :])
            self.t_fw = tl
            tmb = self.mm([tl, t_bv], self.psum[5][:, 0:128], self.ones_rf[:, :],
                          self.F(self.F_W, 128)[0:1, :], True, True, inc="pe")
            self.t_fw = tmb
            t_bv = k.op("dve", lambda e, s4=s4: e.tensor_copy(
                self.B(A_BB + 128 * s4, 128), self.psum[5][:, 0:128]), deps=[tmb], inc="dve")
            self.t_bvrow_free = t_bv
        tkn = None
        t_ks = None
        t_kw = []
        for m in range(Nd):
            if m < 7:
                ws, tw = self.stream_w(bar, "kc", m, 256, m % 2)
                for n4 in range(4):
                    pt = self.psum[2]
                    lastmm = None
                    for kc in range(2):
                        lastmm = self.mm([tw, t_ks, t_ckv[kc * 4 + n4]], pt[:, :],
                                         ws[:, kc * 128:(kc + 1) * 128],
                                         self.B(A_CKV[kc] + 512 * n4, 512),
                                         start=(kc == 0), stop=(kc == 1),
                                         inc="pe" if kc == 1 else None)
                    t_ks = k.op("dve", lambda e, m=m, n4=n4, pt=pt: self.badd(
                        e, self.B(A_KS[m % 2] + 512 * n4, 512), pt[:, :],
                        self.bias_s[:, BQ["b_kc"] + m:BQ["b_kc"] + m + 1]),
                        deps=[lastmm, t_kw[m - 2] if m >= 2 else None], inc="dve")
            else:
                ws, tw = self.stream_w(bar, "kr", 0, 1024, 0)
                ws2, tw2 = self.stream_w(bar, "kr_p", 0, 1024, 1)
                for n4 in range(4):
                    pt, pt2 = self.psum[2], self.psum[3]
                    l1 = l2 = None
                    for kc in range(Nd):
                        l1 = self.mm([tw, t_ks], pt[:, :], ws[:, kc * 128:(kc + 1) * 128],
                                     self.X[kc][:, win + 512 * n4:win + 512 * (n4 + 1)],
                                     start=(kc == 0), stop=(kc == Nd - 1),
                                     inc="pe" if kc == Nd - 1 else None)
                    for kc in range(Nd):
                        l2 = self.mm([tw2], pt2[:, :], ws2[:, kc * 128:(kc + 1) * 128],
                                     self.X[kc][:, win + 512 * n4:win + 512 * (n4 + 1)],
                                     start=(kc == 0), stop=(kc == Nd - 1),
                                     inc="pe" if kc == Nd - 1 else None)
                    ta = k.op("dve", lambda e, pt=pt: self.badd(
                        e, self.B(A_RT[0], 512), pt[:, :],
                        self.bias_s[:, BQ["b_kr"]:BQ["b_kr"] + 1]),
                        deps=[l1, self.t_rt_free], inc="dve")
                    tb2 = k.op("dve", lambda e, pt2=pt2: self.badd(
                        e, self.B(A_RT[1], 512), pt2[:, :],
                        self.bias_s[:, BQ["b_kr_p"]:BQ["b_kr_p"] + 1]),
                        deps=[l2], inc="dve")
                    tr1 = k.op("dve", lambda e, n4=n4: e.tensor_mul(
                        self.B(A_KS[1] + 512 * n4, 512), self.B(A_RT[0], 512),
                        self.B(self.A_COS + 512 * n4, 512)),
                        deps=[ta, self.t_cossin, t_kw[m - 2] if m >= 2 else None], inc="dve")
                    tr2 = k.op("dve", lambda e, n4=n4: e.tensor_mul(
                        self.B(A_RT[0], 512), self.B(A_RT[1], 512),
                        self.B(self.A_SIN + 512 * n4, 512)), deps=[tb2, tr1], inc="dve")
                    t_ks = k.op("dve", lambda e, n4=n4: e.tensor_add(
                        self.B(A_KS[1] + 512 * n4, 512), self.B(A_KS[1] + 512 * n4, 512),
                        self.B(A_RT[0], 512)), deps=[tr2], inc="dve")
                    self.t_rt_free = t_ks
            if m < 7:
                self.strip_done(m % 2, t_ks)
            else:
                self.strip_done(0, l1)
                self.strip_done(1, l2)
            if m % 2 == 1:
                t_gel = []
                for half in range(2):
                    tg = k.op("act", lambda e, half=half, A_KS=A_KS: e.activation(
                        self.B(A_KS[half], 2048), self.B(A_KS[half], 2048), ACT.Gelu),
                        deps=[t_ks], inc="act")
                    t_gel.append(tg)
                for half in range(2):
                    strip = m - 1 + half
                    for n4 in range(4):
                        tsq = k.op("act", lambda e, half=half, n4=n4, A_KS=A_KS: e.activation(
                            self.B(self.A_SG, 512), self.B(A_KS[half] + 512 * n4, 512), ACT.Square),
                            deps=[t_gel[half], tkn], inc="act")
                        tq2_ = self.mm([tsq], self.psum[6][0:16, :],
                                       self.blockones[:, strip * 16:(strip + 1) * 16],
                                       self.B(self.A_SG, 512), True, True, inc="pe")
                        tkm = k.op("dve", lambda e: e.reduce_max(self.knmax[:, 0:1],
                                                                 self.psum[6][0:16, :], AX),
                                   deps=[tq2_], inc="dve")
                        if tkn is None and strip == 0 and n4 == 0:
                            tkn = k.op("dve", lambda e: e.tensor_copy(self.knmax[:, 1:2],
                                                                      self.knmax[:, 0:1]),
                                       deps=[tkm], inc="dve")
                        else:
                            tkn = k.op("dve", lambda e: e.tensor_max(self.knmax[:, 1:2],
                                                                     self.knmax[:, 1:2],
                                                                     self.knmax[:, 0:1]),
                                       deps=[tkm, tkn], inc="dve")
                tw1 = self.dma([t_gel[0], tkn], self.scr["kk"][m - 1],
                               self.B(A_KS[0], 2048), sem="dw")
                tw2_ = self.dma([t_gel[1], tkn], self.scr["kk"][m],
                                self.B(A_KS[1], 2048), sem="dw")
                t_kw.append(tw2_)
                t_kw.append(tw2_)
                t_ks = tw2_
        self.t_knmax = tkn
        self.t_kscr = t_kw[-1]
        wsv0, twv0 = self.stream_w(bar, "v", 0, 1024, 0)
        wsv1, twv1 = self.stream_w(bar, "v", 1, 1024, 1)
        t_vprev = [None, None]
        t_evprev = [None, None]
        for tt in range(16):
            vb = A_VT[tt % 2]
            lastmm = [None, None]
            for n2 in range(2):
                pt = self.psum[n2]
                lmm = None
                for kc in range(2):
                    lmm = self.mm([twv0, twv1, t_vprev[tt % 2], t_evprev[n2],
                                   t_ckv[kc * 4 + tt // 4]],
                                  pt[:, :], self.B(A_CKV[kc] + 128 * tt, 128),
                                  (wsv0 if kc == 0 else wsv1)[:, 512 * n2:512 * (n2 + 1)],
                                  start=(kc == 0), stop=(kc == 1),
                                  inc="pe" if kc == 1 else None)
                lastmm[n2] = lmm
            tev1 = k.op("dve", lambda e, vb=vb: e.tensor_add(
                self.BIG[:, vb:vb + 520].rearrange("p (h c) -> p h c", c=65)[:, :, 0:64],
                self.psum[0][:, :].rearrange("p (h c) -> p h c", c=64),
                self.B(A_BB, 512).rearrange("p (h c) -> p h c", c=64)),
                deps=[lastmm[0], t_bv, t_vprev[tt % 2]], inc="dve")
            tev2 = k.op("dve", lambda e, vb=vb: e.tensor_add(
                self.BIG[:, vb + 520:vb + 1040].rearrange("p (h c) -> p h c", c=65)[:, :, 0:64],
                self.psum[1][:, :].rearrange("p (h c) -> p h c", c=64),
                self.B(A_BB + 512, 512).rearrange("p (h c) -> p h c", c=64)),
                deps=[lastmm[1], t_bv, tev1], inc="dve")
            tos = k.op("dve", lambda e, vb=vb: e.memset(
                self.BIG[:, vb:vb + 1040].rearrange("p (h c) -> p h c", c=65)[:, :, 64:65], 1.0),
                deps=[tev2], inc="dve")
            t_evprev = [tev1, tev2]
            tvw = self.dma([tos], self.scr["vv"][tt], self.BIG[:, vb:vb + 1040], sem="dw")
            t_vprev[tt % 2] = tvw
        self.strip_done(0, lastmm[1])
        self.strip_done(1, lastmm[1])
        self.t_vscr = [t_vprev[0], t_vprev[1]]
        self.bar = k.barrier_tokens()

    def emit_l1_chunk(self, b, c2):
        k, D_ = self.k, self.D
        bar = self.bar
        win = b * S + c2 * FCH
        BQ = self.bias_cols
        A_CQ = self.A_CQv
        A_QT = [self.A_W[0] + 512 * i for i in range(Nd)]
        A_AT = [self.A_W[0] + 4096 + 512 * i for i in range(Nd)]
        A_KB = [self.A_SC[0] + 256 * i for i in range(2)]
        A_VB = [self.A_SC[0] + 512 + 260 * i for i in range(2)]
        A_P = [self.A_SC[0] + 1040 + 512 * i for i in range(2)]
        A_R1 = self.A_BC[0]
        A_R2 = self.A_BC[0] + 512
        A_EO = self.A_BC[1]
        t_q = [None] * Nd
        for m in range(7):
            ws, tw = self.stream_w(bar, "qc", m, 256, m % 2)
            pt = self.psum[6]
            lastmm = None
            for kc in range(2):
                lastmm = self.mm([tw, t_q[m - 1] if m else None] + bar, pt[:, :],
                                 ws[:, kc * 128:(kc + 1) * 128],
                                 self.B(A_CQ[kc] + 512 * c2, 512),
                                 start=(kc == 0), stop=(kc == 1), inc="pe" if kc == 1 else None)
            t_q[m] = k.op("dve", lambda e, m=m, pt=pt: self.badd(
                e, self.B(A_QT[m], 512), pt[:, :],
                self.bias_s[:, BQ["b_qc"] + m:BQ["b_qc"] + m + 1]),
                deps=[lastmm] + bar, inc="dve")
            self.strip_done(m % 2, lastmm)
        ws, tw = self.stream_w(bar, "qr", 0, 1024, 0)
        ws2, tw2 = self.stream_w(bar, "qr_p", 0, 1024, 1)
        pt, pt2 = self.psum[6], self.psum[7]
        t_pre = None
        if globals().get('DEV_TAP') and b == 0 and c2 == 0:
            pre_dbg = self.nc.alloc_sbuf_tensor("pre_dbg", [128, 512], F32).ap()
            pre_h = self.nc.dram_tensor("pre_h", [128, 512], F32, kind="ExternalOutput").ap()
            tpv = k.op("dve", lambda e, pt=pt: e.tensor_copy(pre_dbg[:, :], pt[:, :]),
                       deps=[t_q[6]], inc="dve")
            t_pre = self.dma([tpv], pre_h[:, :], pre_dbg[:, :], sem="dm")
        l1 = l2 = None
        for kc in range(Nd):
            l1 = self.mm([tw, t_q[6], t_pre], pt[:, :], ws[:, kc * 128:(kc + 1) * 128],
                         self.X[kc][:, win:win + 512],
                         start=(kc == 0), stop=(kc == Nd - 1), inc="pe" if kc == Nd - 1 else None)
        for kc in range(Nd):
            l2 = self.mm([tw2], pt2[:, :], ws2[:, kc * 128:(kc + 1) * 128],
                         self.X[kc][:, win:win + 512],
                         start=(kc == 0), stop=(kc == Nd - 1), inc="pe" if kc == Nd - 1 else None)
        self.strip_done(0, l1)
        self.strip_done(1, l2)
        if globals().get('DEV_TAP') and b == 0 and c2 == 0:
            qr_dbg = self.nc.alloc_sbuf_tensor("qr_dbg", [128, 512], BF16).ap()
            ta0 = k.op("dve", lambda e, pt=pt: self.badd(e, qr_dbg[:, :], pt[:, :],
                                                  self.bias_s[:, BQ["b_qr"]:BQ["b_qr"] + 1]),
                       deps=[l1] + bar, inc="dve")
            ta = k.op("dve", lambda e: e.tensor_copy(self.B(A_R1, 512), qr_dbg[:, :]),
                      deps=[ta0], inc="dve")
        else:
            ta = k.op("dve", lambda e, pt=pt: self.badd(e, self.B(A_R1, 512), pt[:, :],
                                                 self.bias_s[:, BQ["b_qr"]:BQ["b_qr"] + 1]),
                      deps=[l1] + bar, inc="dve")
        tb2 = k.op("dve", lambda e, pt2=pt2: self.badd(e, self.B(A_R2, 512), pt2[:, :],
                                              self.bias_s[:, BQ["b_qr_p"]:BQ["b_qr_p"] + 1]),
                   deps=[l2] + bar, inc="dve")
        if globals().get('DEV_TAP') and b == 0 and c2 == 0:
            if not hasattr(self, "dbg_h"):
                self.dbg_h = self.nc.dram_tensor("dbg", [4, 128, 512], F32, kind="ExternalOutput").ap()
                self.dbgf = self.nc.alloc_sbuf_tensor("dbgf", [128, 512], F32).ap()
            self.t_dbg = getattr(self, "t_dbg", None)
            taps = ((self.B(A_R1, 512), ta, "copy"), (pt[:, :], ta, "copy"),
                    (pt[:, :], ta, "ts"), (pt[:, :], ta, "ts_imm"))
            for di, (src_ap, dtok, mode) in enumerate(taps):
                if mode == "copy":
                    tcv = k.op("dve", lambda e, src_ap=src_ap: e.tensor_copy(self.dbgf, src_ap),
                               deps=[dtok, self.t_dbg], inc="dve")
                elif mode == "ts":
                    tcv = k.op("dve", lambda e, src_ap=src_ap: e.tensor_scalar(
                        self.dbgf, src_ap, self.bias_s[:, 9:10], None, OP.add),
                        deps=[dtok, self.t_dbg], inc="dve")
                else:
                    tcv = k.op("dve", lambda e, src_ap=src_ap: e.tensor_scalar(
                        self.dbgf, src_ap, 0.25, None, OP.add),
                        deps=[dtok, self.t_dbg], inc="dve")
                self.t_dbg = self.dma([tcv], self.dbg_h[di], self.dbgf, sem="dm")
        tr1 = k.op("dve", lambda e: e.tensor_mul(self.B(A_QT[7], 512), self.B(A_R1, 512),
                                                 self.B(self.A_COS + 512 * c2, 512)),
                   deps=[ta, self.t_cossin, getattr(self, "t_dbg", None)] + bar, inc="dve")
        tr2 = k.op("dve", lambda e: e.tensor_mul(self.B(A_R1, 512), self.B(A_R2, 512),
                                                 self.B(self.A_SIN + 512 * c2, 512)),
                   deps=[tb2, tr1], inc="dve")
        t_q[7] = k.op("dve", lambda e: e.tensor_add(self.B(A_QT[7], 512), self.B(A_QT[7], 512),
                                                    self.B(A_R1, 512)), deps=[tr2], inc="dve")
        if globals().get('DEV_TAP_OLD') and b == 0 and c2 == 0:
            if not hasattr(self, "dbg_h"):
                self.dbg_h = self.nc.dram_tensor("dbg", [4, 128, 512], F32, kind="ExternalOutput").ap()
                dbf = self.nc.alloc_sbuf_tensor("dbgf", [128, 512], F32).ap()
                self.dbgf = dbf
            for di, col in enumerate((A_QT[7], A_R1, A_R2, self.A_COS + 512 * c2)):
                tcv = k.op("dve", lambda e, col=col: e.tensor_copy(self.dbgf, self.B(col, 512)),
                           deps=[t_q[7], self.t_dbg], inc="dve")
                self.t_dbg = self.dma([tcv], self.dbg_h[di], self.dbgf, sem="dm")
        t_gel = [None] * Nd
        for m in range(Nd):
            t_gel[m] = k.op("act", lambda e, m=m: e.activation(self.B(A_QT[m], 512),
                                                               self.B(A_QT[m], 512), ACT.Gelu),
                            deps=[t_q[m], getattr(self, "t_dbg", None)], inc="act")
        tq2_ = None
        tsq = None
        for m in range(Nd):
            tsq = k.op("act", lambda e, m=m: e.activation(self.B(self.A_SG, 512),
                                                          self.B(A_QT[m], 512), ACT.Square),
                       deps=[t_gel[m], tsq, tq2_], inc="act")
            tq2_ = self.mm([tsq], self.psum[6][0:16, :], self.blockones[:, m * 16:(m + 1) * 16],
                           self.B(self.A_SG, 512), start=(m == 0), stop=(m == Nd - 1), inc="pe")
        tqm = k.op("dve", lambda e: e.reduce_max(self.qnmax[:, 0:1], self.psum[6][0:16, :], AX),
                   deps=[tq2_], inc="dve")
        tm1 = k.op("dve", lambda e: e.tensor_add(self.mrow[:, 0:1], self.qnmax[:, 0:1],
                                                 self.knmax[:, 1:2]),
                   deps=[tqm, self.t_knmax], inc="dve")
        tm2 = k.op("dve", lambda e: e.tensor_scalar(self.mrow[:, 0:1], self.mrow[:, 0:1],
                                                    -1.0 / 16.0, None, OP.mult),
                   deps=[tm1], inc="dve")
        tdm = self.dma([tm2, self.t_negm_mm], self.rowt[:, :], self.mrow[:, :],
                       sem="dm", allow_nc=True)
        tnm_mm = self.mm([tdm, self.t_ones], self.psum[7][:, 0:16], self.ones_rf[:, :],
                         self.rowt[:, :], True, True, inc="pe")
        self.t_negm_mm = tnm_mm
        t_negm = k.op("dve", lambda e: e.tensor_copy(self.negm[:, :], self.psum[7][:, 0:16]),
                      deps=[tnm_mm], inc="dve")
        jmax = 4 * c2 + 3
        t_at = [None] * Nd
        t_rc = None
        t_eo = None
        t_hg_prev = None
        for hg in range(4):
            t_kb = [t_hg_prev, t_hg_prev]
            t_vb = [t_hg_prev, t_hg_prev]
            t_av = [None] * 4
            t_pfree = [None, None]
            pend = None
            av_map = {}

            def flush(pend):
                if pend is None:
                    return
                (pA, vb_, hh_, pbuf_, ji_, j_, tvl_, tE_) = pend
                tAV = self.k.op("pe", lambda e, pA=pA, vb_=vb_, hh_=hh_, pbuf_=pbuf_,
                                ji_=ji_, j_=j_: e.matmul(
                                    pA[0:65, :], vb_[:, 65 * hh_:65 * hh_ + 65],
                                    pbuf_, start=(ji_ == 0), stop=(j_ == 0),
                                    skip_group_check=True),
                                deps=[tvl_, tE_, t_av[hh_]], inc="pe")
                t_av[hh_] = tAV
                t_pfree[(ji_ * 4 + hh_) % 2] = tAV
                av_map[(ji_, hh_)] = tAV

            for ji, j in enumerate(range(jmax, -1, -1)):
                off = max(0, 128 * j - 512 * c2)
                kb = self.B(A_KB[ji % 2], 256)
                vb = self.B(A_VB[ji % 2], 260)
                diag = 128 * j >= 512 * c2
                tkl = self.dma([t_kb[ji % 2], self.t_kscr] + bar,
                               kb.rearrange("p (a j) -> p a j", j=128),
                               self.scr["kk"][2 * hg:2 * hg + 2, :, 128 * j:128 * (j + 1)]
                               .rearrange("a p j -> p a j"), sem="dk%d" % (ji % 2), eng="sp")
                tvl = self.dma([av_map.get((ji - 2, 3), t_hg_prev),
                                self.t_vscr[0], self.t_vscr[1]] + bar, vb,
                               self.scr["vv"][j][:, 260 * hg:260 * (hg + 1)],
                               sem="dv%d" % (ji % 2), eng="sp")
                tS = None
                for hh in range(4):
                    h = 4 * hg + hh
                    pS = self.psum[4 + ((ji * 4 + hh) % 2)]
                    lhsK = kb[64 * (hh % 2):64 * (hh % 2) + 64,
                              128 * (hh // 2):128 * (hh // 2) + 128]
                    rhsQ = self.B(A_QT[h // 2], 512)[64 * (h % 2):64 * (h % 2) + 64, :]
                    tS = self.mm([tkl, t_gel[h // 2], t_pfree[(ji * 4 + hh) % 2]],
                                 pS[:, :], lhsK, rhsQ, True, True, inc="pe")
                    pbuf = self.B(A_P[(ji * 4 + hh) % 2], 512)
                    tE = k.op("act", lambda e, pS=pS, pbuf=pbuf, h=h: e.activation(
                        pbuf, pS[:, :], ACT.Exp,
                        bias=self.negm[:, h:h + 1], scale=0.125),
                        deps=[tS, t_negm], inc="act")
                    if diag:
                        tE = k.op("dve", lambda e, pbuf=pbuf, off=off: e.tensor_mul(
                            pbuf[:, off:off + 128], pbuf[:, off:off + 128], self.tri[:, :]),
                            deps=[tE, self.t_const], inc="dve")
                    if off > 0:
                        tE = k.op("dve", lambda e, pbuf=pbuf, off=off: e.memset(
                            pbuf[:, 0:off], 0.0), deps=[tE], inc="dve")
                    flush(pend)
                    pend = (self.psum[hh], vb, hh, pbuf, ji, j, tvl, tE)
                t_kb[ji % 2] = tS
            flush(pend)
            pend = None
            t_hg_prev = t_av[3]
            for hh in range(4):
                h = 4 * hg + hh
                pA = self.psum[hh]
                def _recip(e, pA=pA):
                    with self.nc.allow_low_precision(reason="attn denom recip to bf16"):
                        return e.reciprocal(self.trcp[64:65, :], pA[64:65, :])
                trc = k.op("dve", _recip, deps=[t_av[hh], self.t_trcp0, t_rc], inc="dve")
                tbc = self.mm([trc], self.psum[6][0:64, :], self.e64[:, :], self.trcp[0:65, :],
                              True, True, inc="pe")
                trb = k.op("dve", lambda e: e.tensor_copy(self.B(self.A_Q, 512)[0:64, :],
                                                          self.psum[6][0:64, :]),
                           deps=[tbc], inc="dve")
                if h % 2 == 0:
                    t_at[h // 2] = k.op("dve", lambda e, pA=pA, h=h: e.tensor_mul(
                        self.B(A_AT[h // 2], 512)[0:64, :], pA[0:64, :],
                        self.B(self.A_Q, 512)[0:64, :]),
                        deps=[trb, t_at[h // 2]] + bar, inc="dve")
                    t_rc = t_at[h // 2]
                else:
                    teo = k.op("dve", lambda e, pA=pA: e.tensor_mul(
                        self.B(A_EO, 512)[0:64, :], pA[0:64, :], self.B(self.A_Q, 512)[0:64, :]),
                        deps=[trb, t_eo] + bar, inc="dve")
                    tdo = self.dma([teo, t_at[h // 2]],
                                   self.BIG[64:128, A_AT[h // 2]:A_AT[h // 2] + 512],
                                   self.B(A_EO, 512)[0:64, :], sem="dm")
                    t_at[h // 2] = tdo
                    t_eo = tdo
                    t_rc = teo
        t_xs = [None] * Nd
        for m in range(Nd):
            ws, tw = self.stream_w(bar + [t_rc], "o", m, 1024, m % 2)
            pt = self.psum[4 + m % 2]
            lastmm = None
            for kc in range(Nd):
                lastmm = self.mm([tw, t_at[kc], t_xs[m - 2] if m >= 2 else None],
                                 pt[:, 0:512], ws[:, kc * 128:(kc + 1) * 128],
                                 self.B(A_AT[kc], 512), start=(kc == 0), stop=(kc == Nd - 1),
                                 inc="pe" if kc == Nd - 1 else None)
            tx1 = k.op("dve", lambda e, m=m, pt=pt, win=win: e.tensor_add(
                self.B(A_QT[m], 512), pt[:, 0:512], self.X[m][:, win:win + 512]),
                deps=[lastmm, t_gel[m], t_xs[m - 1] if m else None], inc="dve")
            txs = k.op("dve", lambda e, m=m: self.badd(
                e, self.B(A_QT[m], 512), self.B(A_QT[m], 512),
                self.bias_s[:, BQ["b_o"] + m:BQ["b_o"] + m + 1]),
                deps=[tx1], inc="dve")
            t_xs[m] = txs
            self.strip_done(m % 2, lastmm)
        xs_tiles = [self.B(A_QT[i], 512) for i in range(Nd)]
        xo_tiles = [self.X[i][:, win:win + 512] for i in range(Nd)]
        self.emit_layernorm(t_xs, xs_tiles, xo_tiles)
        self.bar = k.barrier_tokens()

    def emit_ffn1(self, b, c2):
        k = self.k
        win = b * S + c2 * FCH
        t0 = c2 * FCH
        bar = self.bar
        A_H = [self.R0 + 512 * i for i in range(24)]
        A_XS = [self.A_Z[0] + 512 * i for i in range(Nd)]
        t_h = [None] * 24
        gbuf = [self.A_SG, self.A_SC[0]]
        ubuf = [self.A_Q, self.A_SC[2]]
        t_sgfree = [bar, bar]
        t_qfree = [bar, bar]
        for m in range(24):
            mp = m % 2
            wsg, twg = self.stream_strip_bf16(bar + [self.t_scr["d1"]], self.scr["g1"], 0, Nd, m, 0)
            wsu, twu = self.stream_strip_bf16(bar + [self.t_scr["d1"]], self.scr["u1"], 0, Nd, m, 1)
            ptg, ptu = self.psum[m % 2], self.psum[2 + m % 2]
            lg = lu = None
            for kc in range(Nd):
                lg = self.mm([twg, t_h[m - 2] if m >= 2 else None] + bar, ptg[:, 0:FCH],
                             wsg[:, kc * 128:(kc + 1) * 128],
                             self.X[kc][:, win:win + FCH], start=(kc == 0), stop=(kc == Nd - 1),
                             inc="pe" if kc == Nd - 1 else None)
            for kc in range(Nd):
                lu = self.mm([twu] + bar, ptu[:, 0:FCH], wsu[:, kc * 128:(kc + 1) * 128],
                             self.X[kc][:, win:win + FCH], start=(kc == 0), stop=(kc == Nd - 1),
                             inc="pe" if kc == Nd - 1 else None)
            tg = k.op("act", lambda e, ptg=ptg, mp=mp: e.activation(
                self.B(gbuf[mp], FCH), ptg[:, 0:FCH], ACT.Silu, scale=1.0 / 127.0),
                deps=[lg, t_sgfree[mp]] + bar, inc="act")
            tu = k.op("act", lambda e, ptu=ptu, mp=mp: e.activation(
                self.B(ubuf[mp], FCH), ptu[:, 0:FCH], ACT.Copy, scale=1.0 / 127.0),
                deps=[lu, t_qfree[mp]] + bar, inc="act")
            th = k.op("dve", lambda e, m=m, mp=mp: e.tensor_mul(
                self.B(A_H[m], FCH), self.B(gbuf[mp], FCH), self.B(ubuf[mp], FCH)),
                deps=[tg, tu] + bar, inc="dve")
            t_sgfree[mp] = th
            t_qfree[mp] = th
            t_h[m] = th
            self.strip_done(0, lg)
            self.strip_done(1, lu)
        ht = [self.B(A_H[m], FCH) for m in range(24)]
        dn_guard = [[self.t_dn_guard[m], list(self.t_ln_ps), bar] for m in range(Nd)]
        last_mm = [None] * Nd
        for kc in range(24):
            wsd, twd = self.stream_w([t_h[kc]], "d1", kc, 1024, 2 + kc % 2)
            for m in range(Nd):
                last_mm[m] = self.mm(
                    [twd, t_h[kc], dn_guard[m] if kc == 0 else None,
                     (t_h[22 + m % 2] if kc == 0 and m < 4 else None)],
                    self.psum[m][:, 0:FCH],
                    wsd[:, m * 128:(m + 1) * 128], ht[kc],
                    start=(kc == 0), stop=(kc == 23),
                    inc="pe" if m == Nd - 1 else None)
            self.strip_done(2 + kc % 2, last_mm[Nd - 1])
        t_xs = [None] * Nd
        for m in range(Nd):
            txs = k.op("dve", lambda e, m=m, win=win: e.scalar_tensor_tensor(
                self.B(A_XS[m], FCH), self.psum[m][:, 0:FCH], 1.0 / 127.0,
                self.X[m][:, win:win + FCH], OP.mult, OP.add),
                deps=[last_mm[Nd - 1], t_xs[m - 1] if m else None] + bar, inc="dve")
            t_xs[m] = txs
        self.t_dn_guard = [t_xs[m] for m in range(Nd)]
        xs_tiles = [self.B(A_XS[i], FCH) for i in range(Nd)]
        xo_tiles = [self.X[i][:, win:win + FCH] for i in range(Nd)]
        self.emit_layernorm(t_xs, xs_tiles, xo_tiles, f32_dma=(b, t0))
        self.bar = k.barrier_tokens()

    def finalize(self):
        k = self.k
        final_st = k.cnt["dst"]
        k.op("sp", lambda e: None, deps=[("dst", final_st)])
        ctx = contextlib.ExitStack()
        for s in k.semnames:
            k.sem[s] = ctx.enter_context(self.nc.semaphore(s))
        with self.nc.Block() as block:
            @block.tensor
            def _(e):
                for fn in k.q["pe"]:
                    fn(e)

            @block.vector
            def _(e):
                for fn in k.q["dve"]:
                    fn(e)

            @block.scalar
            def _(e):
                for fn in k.q["act"]:
                    fn(e)

            @block.gpsimd
            def _(e):
                for fn in k.q["pool"]:
                    fn(e)

            @block.sync
            def _(e):
                for fn in k.q["sp"]:
                    fn(e)
        ctx.close()
        return self.nc


def _emit_x_dump(p):
    k = p.k
    t = p.bar[-1] if p.bar else None
    tprev = None
    for b in range(B_CORE):
        for i in range(Nd):
            for w4 in range(4):
                tc = k.op("act", lambda e, b=b, i=i, w4=w4: e.copy(
                    p.F(p.F_Y[0], 512), p.X[i][:, b * S + 512 * w4:b * S + 512 * (w4 + 1)]),
                    deps=p.bar + [tprev], inc="act")
                tprev = p.dma([tc], p.out_h[b, 128 * i:128 * (i + 1), 512 * w4:512 * (w4 + 1)],
                              p.F(p.F_Y[0], 512), sem="dst")
    p.t_out_last = tprev


def build_program():
    p = Prog()
    p.t_rt_free = None
    p.t_bvrow_free = None
    p.t_sp_prev = None
    p.t_ln_ps = ()
    p.t_ln_app = None
    p.t_dn_guard = [None] * 8
    p.t_negm_mm = None
    p.t_dbg = None
    p.emit_init()
    p.emit_weight_prep()
    for b in range(B_CORE):
        for c in range(2):
            p.emit_l0_chunk(b, c)
        if globals().get('DEV_STAGE') == "l0":
            continue
        for c2 in range(4):
            p.emit_ffn0(b, c2)
    if globals().get('DEV_STAGE') in ("l0", "ffn0"):
        _emit_x_dump(p)
        p.finalize()
        return p
    for b in range(B_CORE):
        p.emit_l1_kv(b)
        if globals().get('DEV_STAGE') == "kv":
            break
        for c2 in range(4):
            p.emit_l1_chunk(b, c2)
        if globals().get('DEV_STAGE') == "attn" and b == 0:
            break
        for c2 in range(4):
            p.emit_ffn1(b, c2)
    if globals().get('DEV_STAGE') in ("kv", "attn"):
        _emit_x_dump(p)
    p.finalize()
    return p


def kernel(**inputs):
    p = build_program()
    in_names = list(p.D.keys())
    in_maps = []
    for core in range(NCORES):
        hm = host_prep(inputs, core)
        in_maps.append({"in_" + n: np.ascontiguousarray(hm[n]) for n in in_names})
    res = run_bass_kernel_spmd(p.nc, in_maps, core_ids=list(range(NCORES)))
    outs = []
    for core in range(NCORES):
        o = np.asarray(res.results[core]["out"])
        outs.append(o.transpose(0, 2, 1))
    return np.ascontiguousarray(np.concatenate(outs, axis=0)).astype(np.float32)
